# version 09
# baseline (speedup 1.0000x reference)
import os
import numpy as np
import ml_dtypes
import concourse.bass as bass
import concourse.mybir as mybir
from concourse.bass_utils import run_bass_kernel_spmd
from contextlib import ExitStack

F32 = mybir.dt.float32
BF16 = mybir.dt.bfloat16
AL = mybir.AluOpType
AF = mybir.ActivationFunctionType
AX = mybir.AxisListType.X

MAGIC = 12582912.0
DECAY = 0.995
D = 1024
HID = 4096
T = 4096
G = 512
NG = T // G
C = 128
NCH = G // C
SCL = float(D) ** -0.5
EPS = 1e-5
DC = float(DECAY ** C)
NHC = HID // 128
KD = D // 128


class Prog:
    ENGINES = ("tensor", "vector", "scalar", "gpsimd", "sync")

    def __init__(self):
        self.ops = {e: [] for e in self.ENGINES}
        self.events = {}
        self.counts = {e: 0 for e in self.ENGINES}
        self.dma_counts = {}
        self.dma_events = {}

    def op(self, eng, fn, waits=(), event=None, dma=False, stream=None):
        if isinstance(waits, str):
            waits = (waits,)
        waits = tuple(w for w in waits if w is not None)
        if dma:
            if stream is None:
                stream = "q_" + eng
            self.dma_counts[stream] = self.dma_counts.get(stream, 0) + 16
            if event is not None:
                assert event not in self.dma_events, event
                self.dma_events[event] = (stream, self.dma_counts[stream])
        elif event is not None:
            self.counts[eng] += 1
            assert event not in self.events, event
            self.events[event] = (eng, self.counts[eng])
        self.ops[eng].append((fn, waits, event, dma, stream))
        return event

    DRAIN_OPS = {"tensor_copy", "tensor_tensor", "tensor_add", "tensor_sub",
                 "tensor_mul", "tensor_max", "tensor_scalar", "tensor_single_scalar",
                 "tensor_scalar_add", "tensor_scalar_mul", "tensor_scalar_max",
                 "tensor_scalar_min", "tensor_reduce", "reduce_max", "reduce_sum",
                 "scalar_tensor_tensor", "reciprocal", "memset", "memzero",
                 "activation", "sign", "square", "sqrt", "mul", "copy", "add",
                 "select", "transpose", "bn_stats", "bn_aggr"}

    class _DrainProxy:
        def __init__(self, eng):
            self._e = eng

        def __getattr__(self, name):
            f = getattr(self._e, name)
            if name in Prog.DRAIN_OPS:
                eng = self._e

                def wrapped(*a, **k):
                    eng.drain()
                    return f(*a, **k)
                return wrapped
            return f

    def emit(self, block, sems, dma_sems):
        for eng_name in self.ENGINES:
            ops = self.ops[eng_name]

            def make(eng_name, ops):
                def body(eng):
                    if eng_name in ("vector", "scalar"):
                        eng = Prog._DrainProxy(eng)
                    last = {}
                    for fn, waits, event, dma, stream in ops:
                        for w in waits:
                            if w in self.events:
                                e2, val = self.events[w]
                                if e2 == eng_name:
                                    continue
                                sem, key = sems[e2], (e2, 0)
                            else:
                                e2, val = self.dma_events[w]
                                sem, key = dma_sems[e2], (e2, 1)
                            if last.get(key, -1) < val:
                                eng.wait_ge(sem, val)
                                last[key] = val
                        instr = fn(eng)
                        if event is not None:
                            instr.then_inc(dma_sems[stream] if dma else sems[eng_name],
                                           16 if dma else 1)
                return body

            getattr(block, eng_name)(make(eng_name, ops))


def _last(x):
    return x[-1]


def build_program():
    nc = bass.Bass("TRN2", target_bir_lowering=False)
    P = Prog()
    dr = {}

    def din(name, shape, dtype=F32):
        dr[name] = nc.dram_tensor(name, shape, dtype, kind="ExternalInput").ap()

    def dout(name, shape, dtype=F32):
        dr[name] = nc.dram_tensor(name, shape, dtype, kind="ExternalOutput").ap()

    DBG = os.environ.get("KMEGA_DEBUG", "")
    dbg_names = set(DBG.split(",")) if DBG else set()

    def dint(name, shape, dtype=F32):
        kind = "ExternalOutput" if name in dbg_names else "Internal"
        dr[name] = nc.dram_tensor(name, shape, dtype, kind=kind).ap()

    din("u_pad", [T + 2, D])
    din("w1T", [D, HID]); din("w2T", [HID, D])
    din("cw_cols", [128, NHC * 3]); din("cb_cols", [128, NHC])
    din("wvT", [D, D]); din("wmvT", [D, D]); din("wqT", [D, D]); din("wmqT", [D, D])
    din("whT", [D, D]); din("wrmT", [2 * D, D])
    din("gTp", [128, KD * 26])
    din("bias_rows", [8, D])
    din("gateb_row", [1, 32])
    din("ident", [128, 128]); din("identb", [128, 128], BF16)
    din("onesr", [1, 128])
    din("selb", [8, 5 * 128], BF16)
    din("minc", [C, C], BF16); din("mdec", [C, C], BF16)
    din("dec1", [C, 1]); din("decC", [C, 1])
    din("flags", [1, 4])
    dout("out", [T, D])

    dint("wq1b_s", [128, KD * HID], BF16)
    dint("wq2_s", [HID, D], BF16)
    for n in ("wv_s", "wmv_s", "wq_s", "wmq_s", "wh_s"):
        dint(n, [D, D], BF16)
    dint("wrm_s", [2 * D, D], BF16)
    dint("x1_s", [T, D], F32)
    dint("pre_s", [T, D], BF16)
    for n in ("rv_s", "mv_s", "rq_s", "mq_s"):
        dint(n, [T, D], BF16)
    dint("tsc", [1, 4], F32)
    dint("s2row", [4, 128], F32)
    dint("dbg1", [128, 8], F32)
    dint("dbg2", [128, 24], F32)
    dint("dbg3", [128, D], F32)
    dint("cc1_in", [24, D], F32); dint("cc1_out", [24, D], F32)
    dint("cc2_in", [2, D], F32); dint("cc2_out", [2, D], F32)
    dint("ccb_in", [1, 16], F32); dint("ccb_out", [1, 16], F32)

    es = ExitStack()

    def sbuf(name, shape, dtype=F32):
        return es.enter_context(nc.sbuf_tensor(name, shape, dtype))

    ident = sbuf("c_id", [128, 128], F32)
    identb = sbuf("c_idb", [128, 128], BF16)
    onesr = sbuf("c_1r", [1, 128], F32)
    sel_b = sbuf("c_selb", [8, 5 * 128], BF16)
    bias_tb = sbuf("c_bib", [8, D], BF16)
    minc_t = sbuf("c_mi", [C, C], BF16)
    mdec_t = sbuf("c_md", [C, C], BF16)
    dec1_t = sbuf("c_d1", [C, 1], F32)
    decC_t = sbuf("c_dC", [C, 1], F32)
    flags_t = sbuf("c_fl", [128, 4], F32)
    bias_t = sbuf("c_bi", [8, D], F32)
    gateb_t = sbuf("c_gb", [1, 32], F32)
    gb_b = sbuf("c_gbb", [128, 32], F32)
    lng_b = sbuf("c_lng", [128, D], F32)
    lnb_b = sbuf("c_lnb", [128, D], F32)
    cw_t = sbuf("c_cw", [128, NHC * 3], F32)
    cb_t = sbuf("c_cb", [128, NHC], F32)
    gt_t = sbuf("c_gt", [128, KD * 26], BF16)
    tsc_t = sbuf("c_ts", [128, 4], F32)
    coeff_r = sbuf("r_cr", [128, (T // C) * 8], BF16)
    coeff_m = sbuf("r_cm", [128, (T // C) * 16], BF16)
    statsv = sbuf("r_sv", [128, (T // C) * 2], F32)
    st_r = sbuf("r_str", [8, D], F32)
    st_m = sbuf("r_stm", [16, D], F32)
    st_rb = sbuf("r_strb", [8, D], BF16)
    st_mb = sbuf("r_stmb", [16, D], BF16)
    st_fm = sbuf("r_sfm", [128, KD * 24], F32)
    st_fmb = sbuf("r_sfmb", [128, KD * 24], BF16)
    dq2sv = sbuf("r_dq2", [128, 4], F32)

    wstr = sbuf("w_s", [128, 2 * KD * 128], BF16)
    w2str = sbuf("w_2s", [128, 8 * D], BF16)
    xg = [sbuf(f"b_x{i}", [128, 4 * D], F32) for i in range(2)]
    h_tm = sbuf("b_h", [128, D], F32)
    pool2k = sbuf("b_p2k", [128, 1024], BF16)
    xq_fm = [sbuf(f"b_q{i}", [128, KD * (G + 2)], BF16) for i in range(2)]
    hid_t = sbuf("b_hid", [128, NHC * G], BF16)
    hmax = sbuf("b_hmx", [128, G], F32)
    hmin = sbuf("b_hmn", [128, G], F32)
    scr2 = sbuf("b_sc2", [128, G], F32)
    qy = sbuf("b_qy", [128, D], F32)
    xqt = sbuf("b_xqt", [128, D], BF16)
    stt = sbuf("b_st", [128, 32], F32)
    qstt = sbuf("b_qst", [2, 32], F32)
    rowsA = sbuf("b_rA", [128, D], F32)
    estt = sbuf("b_estt", [128, 32], F32)
    projb = sbuf("b_pb", [128, D], BF16)
    gsb = sbuf("b_gs", [128, 32], F32)
    rv_tm = sbuf("p_rv", [128, 2 * D], BF16)
    mv_tm = sbuf("p_mv", [128, 2 * D], BF16)
    rv_fmt = sbuf("p_rvf", [128, KD * G], BF16)
    mv_fmt = sbuf("p_mvf", [128, KD * G], BF16)
    rq_fmt = sbuf("p_rqf", [128, KD * G], BF16)
    mq_fmt = sbuf("p_mqf", [128, KD * G], BF16)
    pre_t = sbuf("p_pr", [128, 2 * D], BF16)
    p_tm = sbuf("p_p", [128, 24], F32)
    pms = sbuf("p_pm", [128, 16], F32)
    pbf = sbuf("p_pbf", [128, 24], BF16)
    pmsbf = sbuf("p_pmb", [128, 16], BF16)
    p_fm = sbuf("p_pf", [24, C], BF16)
    pms_fm = sbuf("p_pmf", [16, C], BF16)
    pmr_fm = sbuf("p_pmr", [16, C], BF16)
    crf = sbuf("p_crf", [8, C], BF16)
    cmf = sbuf("p_cmf", [16, C], BF16)
    cmd_b = sbuf("p_cd", [128, 16], BF16)
    sc_m = sbuf("p_sm", [128, 16], F32)
    z1 = sbuf("p_z1", [128, D], F32)
    wpart = sbuf("w_pt", [128, 44], F32)

    h_fm = pool2k
    reads_sb = pool2k
    ebf = pool2k
    a_sb_o = 512
    w_sb_o = 768
    pb = [es.enter_context(nc.psum_tensor(f"pb{i}", [128, 512], F32)) for i in range(5)]
    pwide = es.enter_context(nc.psum_tensor("pwide", [128, 1024], F32))
    pbT = es.enter_context(nc.psum_tensor("pbT", [128, 1024], BF16))

    ev_n = [0]
    shared_w = [None]

    def E(tag=""):
        ev_n[0] += 1
        return f"e{ev_n[0]}{tag}"

    OP = P.op

    cl = []
    for dst, src in [(ident, "ident"), (identb, "identb"), (onesr, "onesr"), (sel_b, "selb"),
                     (minc_t, "minc"), (mdec_t, "mdec"), (dec1_t, "dec1"),
                     (decC_t, "decC"), (bias_t, "bias_rows"), (gateb_t, "gateb_row"),
                     (cw_t, "cw_cols"), (cb_t, "cb_cols")]:
        cl.append(OP("sync", lambda s, d=dst, n=src: s.dma_start(out=d[:], in_=dr[n][:]),
                     event=E(), dma=True, stream="c"))
    cl.append(OP("sync", lambda s: s.dma_start(
        out=flags_t[:], in_=dr["flags"][0:1, :].broadcast_to([128, 4])),
        event=E(), dma=True, stream="c"))
    cl.append(OP("sync", lambda s: s.dma_start(
        out=w2str[:, 0:2 * KD * 26].bitcast(F32), in_=dr["gTp"][:, :]), event=E(), dma=True, stream="c"))
    e_gtc = OP("vector", lambda v: v.tensor_copy(gt_t[:, :], w2str[:, 0:2 * KD * 26].bitcast(F32)),
               waits=[cl[-1]], event=E())
    cl.append(OP("sync", lambda s: s.dma_start(
        out=lng_b[:, :], in_=dr["bias_rows"][5:6, :].broadcast_to([128, D])),
        event=E(), dma=True, stream="c"))
    cl.append(OP("sync", lambda s: s.dma_start(
        out=lnb_b[:, :], in_=dr["bias_rows"][6:7, :].broadcast_to([128, D])),
        event=E(), dma=True, stream="c"))
    cl.append(OP("sync", lambda s: s.dma_start(
        out=gb_b[:, :], in_=dr["gateb_row"][0:1, :].broadcast_to([128, 32])),
        event=E(), dma=True, stream="c"))
    e_btb = OP("vector", lambda v: v.tensor_copy(bias_tb[:, :], bias_t[:, :]),
               waits=cl, event=E())
    e_bcmm = OP("tensor", lambda t: t.nop(), waits=cl + [e_btb], event=E())
    e_bcev = OP("vector", lambda v: _last([
        v.memset(st_fm[:, :], 0.0),
        v.memset(st_fmb[:, :], 0.0),
        v.memset(rowsA[:, :], 0.0),
        v.memset(stt[:, :], 0.0),
    ]), waits=[e_bcmm], event=E())

    absdone = []
    for wi, (wname, rows, cols) in enumerate([("w1T", D, HID), ("w2T", HID, D)]):
        for chk in range(rows // 128):
            e_l = OP("sync", lambda s, wn=wname, ck=chk, cn=cols: s.dma_start(
                out=w2str[:, 0:2 * cn].bitcast(F32),
                in_=dr[wn][ck * 128:(ck + 1) * 128, :]),
                waits=[absdone[-1]] if absdone else [e_gtc], event=E(), dma=True, stream="prep")
            col = chk if wi == 0 else 8 + chk
            e_a = OP("scalar", lambda s, cn=cols: s.activation(
                hid_t[:, 0:2 * cn].bitcast(F32), w2str[:, 0:2 * cn].bitcast(F32),
                AF.Abs, bias=0.0, scale=1.0), waits=[e_l] + absdone[-1:], event=E())
            e_r = OP("vector", lambda v, cn=cols, col=col: v.tensor_reduce(
                wpart[:, col:col + 1], hid_t[:, 0:2 * cn].bitcast(F32), AX, AL.add),
                waits=[e_a], event=E())
            absdone.append(e_r)
    e_ones = OP("vector", lambda v: v.memset(wpart[:, 40:41], 1.0),
                waits=[e_bcev], event=E())
    e_sum = OP("vector", lambda v: _last([
        v.tensor_reduce(wpart[:, 41:42], wpart[:, 0:8], AX, AL.add),
        v.tensor_reduce(wpart[:, 42:43], wpart[:, 8:40], AX, AL.add),
    ]), waits=[absdone[-1]], event=E())
    e_tot = OP("tensor", lambda t: t.matmul(pb[0][0:2, 0:1], wpart[:, 41:43],
                                            wpart[:, 40:41], start=True, stop=True),
               waits=[e_sum, e_ones], event=E())
    e_sc = OP("vector", lambda v: v.tensor_scalar(
        qstt[0:2, 20:21], pb[0][0:2, 0:1], 1.0 / (D * HID), 1e-5,
        AL.mult, AL.max), waits=[e_tot], event=E())
    e_scw = OP("gpsimd", lambda g: g.dma_start(
        out=dr["tsc"][0:1, 2:4].rearrange("a b -> b a"), in_=qstt[0:2, 20:21]),
        waits=[e_sc], event=E(), dma=True, stream="tscw")
    e_scb0 = OP("sync", lambda s: s.dma_start(
        out=tsc_t[:], in_=dr["tsc"][0:1, :].broadcast_to([128, 4])),
        waits=[e_scw], event=E(), dma=True, stream="tscb")
    e_scb = OP("vector", lambda v: v.reciprocal(tsc_t[:, 0:2], tsc_t[:, 2:4]),
               waits=[e_scb0], event=E())

    wqdone = []
    for wi, (wname, rows, cols) in enumerate([("w1T", D, HID), ("w2T", HID, D)]):
        for chk in range(rows // 128):
            e_l = OP("sync", lambda s, wn=wname, ck=chk, cn=cols: s.dma_start(
                out=w2str[:, 0:2 * cn].bitcast(F32),
                in_=dr[wn][ck * 128:(ck + 1) * 128, :]),
                waits=[wqdone[-1]] if wqdone else [e_scb], event=E(), dma=True, stream="prep")
            e_q = OP("vector", lambda v, wi=wi, cn=cols: _last([
                v.tensor_scalar(w2str[:, 0:2 * cn].bitcast(F32),
                                w2str[:, 0:2 * cn].bitcast(F32),
                                tsc_t[:, wi:wi + 1], MAGIC, AL.mult, AL.add),
                v.tensor_scalar(w2str[:, 0:2 * cn].bitcast(F32),
                                w2str[:, 0:2 * cn].bitcast(F32),
                                MAGIC, 1.0, AL.subtract, AL.min),
                v.tensor_scalar(w2str[:, 2 * cn:3 * cn] if cn == D else hid_t[:, 0:cn],
                                w2str[:, 0:2 * cn].bitcast(F32), -1.0, None, AL.max),
            ]), waits=[e_l, e_scb], event=E())
            if wi == 0:
                for q in range(4):
                    e_s = OP("gpsimd", lambda g, ck=chk, q=q: g.dma_start(
                        out=dr["wq1b_s"].rearrange("p (c k n) -> p c k n", k=KD, n=128)[
                            :, q * 8:(q + 1) * 8, ck, :],
                        in_=hid_t[:, 0:HID].rearrange("p (c n) -> p c n", n=128)[
                            :, q * 8:(q + 1) * 8, :]),
                        waits=[e_q], event=E(), dma=True, stream="wps")
            else:
                e_s = OP("gpsimd", lambda g, ck=chk: g.dma_start(
                    out=dr["wq2_s"][ck * 128:(ck + 1) * 128, :],
                    in_=w2str[:, 2 * D:3 * D]),
                    waits=[e_q], event=E(), dma=True, stream="wps")
            wqdone.append(e_s)
    for wname, sname, rows in [("wvT", "wv_s", D), ("wmvT", "wmv_s", D),
                               ("wqT", "wq_s", D), ("wmqT", "wmq_s", D),
                               ("whT", "wh_s", D), ("wrmT", "wrm_s", 2 * D)]:
        for chk in range(rows // 128):
            e_l = OP("sync", lambda s, wn=wname, ck=chk: s.dma_start(
                out=w2str[:, 0:2 * D].bitcast(F32),
                in_=dr[wn][ck * 128:(ck + 1) * 128, :]),
                waits=[wqdone[-1]], event=E(), dma=True, stream="prep")
            e_c = OP("vector", lambda v: v.tensor_copy(
                w2str[:, 2 * D:3 * D], w2str[:, 0:2 * D].bitcast(F32)),
                waits=[e_l], event=E())
            e_s = OP("gpsimd", lambda g, sn=sname, ck=chk: g.dma_start(
                out=dr[sn][ck * 128:(ck + 1) * 128, :], in_=w2str[:, 2 * D:3 * D]),
                waits=[e_c], event=E(), dma=True, stream="wps")
            wqdone.append(e_s)
    e_wprep = wqdone[-1]
    shared_w[0] = e_wprep

    last_pbT = [None]

    def quant_tile(tag, np_, src, dst_bf, deps):
        st = stt if np_ == 128 else qstt
        buf = qy if np_ == 128 else src.tensor if False else qy
        dmp = qy if np_ == 128 else rowsA[32:33, :]
        e1 = OP("scalar", lambda s: s.activation(
            dmp[0:np_, 0:D], src, AF.Copy, bias=0.0, scale=1.0 / D,
            accum_out=st[0:np_, 0:1]), waits=deps, event=E(tag))
        e2 = OP("scalar", lambda s: s.activation(
            dmp[0:np_, 0:D], src, AF.Square, bias=0.0, scale=1.0,
            accum_out=st[0:np_, 1:2]), waits=deps + [e1], event=E(tag))
        e3 = OP("vector", lambda v: _last([
            v.tensor_reduce(st[0:np_, 2:3], src, AX, AL.max),
            v.tensor_reduce(st[0:np_, 3:4], src, AX, AL.min),
        ]), waits=deps, event=E(tag))
        e4 = OP("vector", lambda v: _last([
            v.tensor_tensor(st[0:np_, 4:5], st[0:np_, 0:1], st[0:np_, 0:1], AL.mult),
            v.scalar_tensor_tensor(st[0:np_, 5:6], st[0:np_, 1:2], 1.0 / D,
                                   st[0:np_, 4:5], AL.mult, AL.subtract),
        ]), waits=[e1, e2], event=E(tag))
        e5 = OP("scalar", lambda s: s.activation(
            st[0:np_, 6:7], st[0:np_, 5:6], AF.Sqrt, bias=flags_t[0:np_, 2:3], scale=1.0),
            waits=[e4], event=E(tag))
        e6 = OP("vector", lambda v: _last([
            v.reciprocal(st[0:np_, 7:8], st[0:np_, 6:7]),
            v.tensor_sub(st[0:np_, 8:9], st[0:np_, 2:3], st[0:np_, 0:1]),
            v.tensor_sub(st[0:np_, 9:10], st[0:np_, 0:1], st[0:np_, 3:4]),
            v.tensor_max(st[0:np_, 8:9], st[0:np_, 8:9], st[0:np_, 9:10]),
            v.tensor_tensor(st[0:np_, 10:11], st[0:np_, 8:9], st[0:np_, 7:8], AL.mult),
            v.tensor_scalar(st[0:np_, 10:11], st[0:np_, 10:11], 1e-5, None, AL.max),
            v.reciprocal(st[0:np_, 11:12], st[0:np_, 10:11]),
            v.tensor_scalar(st[0:np_, 11:12], st[0:np_, 11:12], 127.0, None, AL.mult),
            v.tensor_tensor(st[0:np_, 12:13], st[0:np_, 7:8], st[0:np_, 11:12], AL.mult),
            v.scalar_tensor_tensor(st[0:np_, 13:14], st[0:np_, 0:1], -1.0,
                                   st[0:np_, 12:13], AL.mult, AL.mult),
            v.tensor_scalar(st[0:np_, 14:15], st[0:np_, 10:11], 1.0 / 127.0,
                            None, AL.mult),
        ]), waits=[e3, e5], event=E(tag))
        e7 = OP("vector", lambda v: v.tensor_scalar(
            buf[0:np_, 0:D], src, st[0:np_, 12:13], st[0:np_, 13:14], AL.mult, AL.add),
            waits=[e6], event=E(tag))
        e8 = OP("vector", lambda v: v.tensor_scalar(
            buf[0:np_, 0:D], buf[0:np_, 0:D], MAGIC, MAGIC, AL.add, AL.subtract),
            waits=[e7], event=E(tag))
        e9 = OP("vector", lambda v: v.tensor_scalar(
            dst_bf, buf[0:np_, 0:D], st[0:np_, 14:15], None, AL.mult),
            waits=[e8] + ([last_pbT[0]] if last_pbT[0] else []), event=E(tag))
        return e9

    def quant_edge(tag, dram_row, deps, mask_col=None):
        st = estt
        e_l = OP("sync", lambda s: s.dma_start(
            out=rowsA[:, :], in_=dram_row.broadcast_to([128, D])),
            waits=deps, event=E(tag), dma=True, stream="edg")
        if mask_col is not None:
            e_l = OP("vector", lambda v: v.tensor_scalar(
                rowsA[:, :], rowsA[:, :], flags_t[:, mask_col:mask_col + 1],
                None, AL.mult), waits=[e_l], event=E(tag))
        e0 = OP("vector", lambda v: _last([
            v.tensor_reduce(st[:, 2:3], rowsA[:, :], AX, AL.max),
            v.tensor_reduce(st[:, 3:4], rowsA[:, :], AX, AL.min),
        ]), waits=[e_l], event=E(tag))
        e1 = OP("scalar", lambda s: _last([
            s.activation(rowsA[:, 0:512], rowsA[:, 0:512], AF.Copy, bias=0.0,
                         scale=1.0, accum_out=st[:, 0:1]),
            s.activation(rowsA[:, 512:1024], rowsA[:, 512:1024], AF.Copy, bias=0.0,
                         scale=1.0, accum_out=st[:, 16:17]),
            s.activation(rowsA[:, 0:512], rowsA[:, 0:512], AF.Square, bias=0.0,
                         scale=1.0, accum_out=st[:, 1:2]),
            s.activation(rowsA[:, 512:1024], rowsA[:, 512:1024], AF.Square, bias=0.0,
                         scale=1.0, accum_out=st[:, 17:18]),
        ]), waits=[e_l, e0], event=E(tag))
        e_rl = OP("sync", lambda s: s.dma_start(
            out=rowsA[:, :], in_=dram_row.broadcast_to([128, D])),
            waits=[e1], event=E(tag), dma=True, stream="edg")
        if mask_col is not None:
            e_rl = OP("vector", lambda v: v.tensor_scalar(
                rowsA[:, :], rowsA[:, :], flags_t[:, mask_col:mask_col + 1],
                None, AL.mult), waits=[e_rl], event=E(tag))
        e3 = OP("vector", lambda v: _last([
            v.tensor_add(st[:, 0:1], st[:, 0:1], st[:, 16:17]),
            v.tensor_add(st[:, 1:2], st[:, 1:2], st[:, 17:18]),
            v.tensor_scalar(st[:, 0:1], st[:, 0:1], 1.0 / D, None, AL.mult),
            v.tensor_tensor(st[:, 4:5], st[:, 0:1], st[:, 0:1], AL.mult),
            v.scalar_tensor_tensor(st[:, 5:6], st[:, 1:2], 1.0 / D,
                                   st[:, 4:5], AL.mult, AL.subtract),
        ]), waits=[e1], event=E(tag))
        e5 = OP("scalar", lambda s: s.activation(
            st[:, 6:7], st[:, 5:6], AF.Sqrt, bias=flags_t[:, 2:3], scale=1.0),
            waits=[e3], event=E(tag))
        e6 = OP("vector", lambda v: _last([
            v.reciprocal(st[:, 7:8], st[:, 6:7]),
            v.tensor_sub(st[:, 8:9], st[:, 2:3], st[:, 0:1]),
            v.tensor_sub(st[:, 9:10], st[:, 0:1], st[:, 3:4]),
            v.tensor_max(st[:, 8:9], st[:, 8:9], st[:, 9:10]),
            v.tensor_tensor(st[:, 10:11], st[:, 8:9], st[:, 7:8], AL.mult),
            v.tensor_scalar(st[:, 10:11], st[:, 10:11], 1e-5, None, AL.max),
            v.reciprocal(st[:, 11:12], st[:, 10:11]),
            v.tensor_scalar(st[:, 11:12], st[:, 11:12], 127.0, None, AL.mult),
            v.tensor_tensor(st[:, 12:13], st[:, 7:8], st[:, 11:12], AL.mult),
            v.scalar_tensor_tensor(st[:, 13:14], st[:, 0:1], -1.0,
                                   st[:, 12:13], AL.mult, AL.mult),
            v.tensor_scalar(st[:, 14:15], st[:, 10:11], 1.0 / 127.0, None, AL.mult),
        ]), waits=[e3, e5], event=E(tag))
        e7 = OP("vector", lambda v: _last([
            v.tensor_scalar(rowsA[:, :], rowsA[:, :], st[:, 12:13], st[:, 13:14],
                            AL.mult, AL.add),
            v.tensor_scalar(rowsA[:, :], rowsA[:, :], MAGIC, MAGIC,
                            AL.add, AL.subtract),
            v.tensor_scalar(ebf[0:1, 0:D], rowsA[0:1, :], st[0:1, 14:15],
                            None, AL.mult),
        ]), waits=[e6, e_rl], event=E(tag))
        return e7

    def transpose_to_fm(tag, np_, src_bf, dstbuf, coff, width, deps, pr_free=()):
        e_t = OP("tensor", lambda t: _last([
            t.transpose(pbT[:, k * 128:k * 128 + np_],
                        src_bf[0:np_, k * 128:(k + 1) * 128], identb[0:np_, 0:np_])
            for k in range(KD)]),
            waits=deps + list(pr_free) + ([last_pbT[0]] if last_pbT[0] else []),
            event=E(tag))
        e_e = OP("vector", lambda v: _last([
            v.tensor_copy(dstbuf[:, k * width + coff: k * width + coff + np_],
                          pbT[:, k * 128:k * 128 + np_]) for k in range(KD)]),
            waits=[e_t], event=E(tag))
        last_pbT[0] = e_e
        return e_e

    def ffn_group(tag, xqb, xq_deps, out_write, prev_class_evs=()):
        W = G + 2
        h_evs = []
        mm_evs = []
        ld_prev = []
        for c in range(NHC):
            boff = (c % 2) * KD * 128
            e_l = OP("sync", lambda s, c=c, boff=boff: s.dma_start(
                out=wstr[:, boff:boff + KD * 128],
                in_=dr["wq1b_s"][:, c * KD * 128:(c + 1) * KD * 128]),
                waits=([mm_evs[c - 2]] if c >= 2 else [e_wprep]) +
                      (list(prev_class_evs) if c < 2 else []),
                event=E(tag), dma=True, stream=f"w1_{c % 2}")
            e_mm = OP("tensor", lambda t, boff=boff: _last([
                _last([t.matmul(pwide[:, 0:512],
                                wstr[:, boff + k * 128:boff + (k + 1) * 128],
                                xqb[:, k * W:k * W + 512],
                                start=(k == 0), stop=(k == KD - 1)),
                       t.matmul(pwide[:, 512:514],
                                wstr[:, boff + k * 128:boff + (k + 1) * 128],
                                xqb[:, k * W + 512:k * W + 514],
                                start=(k == 0), stop=(k == KD - 1))])
                for k in range(KD)]),
                waits=[e_l] + list(xq_deps) + ([h_evs[-1][1]] if h_evs else []),
                event=E(tag))
            mm_evs.append(e_mm)
            e_c1 = OP("scalar", lambda s, c=c: s.activation(
                scr2[:, 0:512], pwide[:, 0:512], AF.Copy, bias=0.0,
                scale=cw_t[:, 3 * c:3 * c + 1]), waits=[e_mm], event=E(tag))
            e_c2 = OP("vector", lambda v, c=c: v.scalar_tensor_tensor(
                scr2[:, 0:512], pwide[:, 1:513], cw_t[:, 3 * c + 1:3 * c + 2],
                scr2[:, 0:512], AL.mult, AL.add), waits=[e_c1], event=E(tag))
            e_c3 = OP("vector", lambda v, c=c: v.scalar_tensor_tensor(
                scr2[:, 0:512], pwide[:, 2:514], cw_t[:, 3 * c + 2:3 * c + 3],
                scr2[:, 0:512], AL.mult, AL.add), waits=[e_c2], event=E(tag))
            e_si = OP("scalar", lambda s, c=c: s.activation(
                hid_t[:, c * G:(c + 1) * G], scr2[:, 0:512], AF.Silu,
                bias=cb_t[:, c:c + 1], scale=tsc_t[:, 2:3]),
                waits=[e_c3], event=E(tag))
            if c == 0:
                e_mx = OP("vector", lambda v: _last([
                    v.tensor_scalar(hmax[:, :], hid_t[:, 0:G], 1.0, None, AL.mult),
                    v.tensor_scalar(hmin[:, :], hid_t[:, 0:G], 1.0, None, AL.mult),
                ]), waits=[e_si], event=E(tag))
            else:
                e_mx = OP("vector", lambda v, c=c: _last([
                    v.tensor_tensor(hmax[:, :], hmax[:, :],
                                    hid_t[:, c * G:(c + 1) * G], AL.max),
                    v.tensor_tensor(hmin[:, :], hmin[:, :],
                                    hid_t[:, c * G:(c + 1) * G], AL.min),
                ]), waits=[e_si, h_evs[-1][0]], event=E(tag))
            h_evs.append((e_mx, e_si))

        e_tr = OP("tensor", lambda t: _last([
            _last([t.transpose(pwide[:, i * 128:(i + 1) * 128],
                               hmax[:, i * 128:(i + 1) * 128], ident[:, :]),
                   t.transpose(pwide[:, 512 + i * 128:512 + (i + 1) * 128],
                               hmin[:, i * 128:(i + 1) * 128], ident[:, :])])
            for i in range(4)]), waits=[h_evs[-1][0], mm_evs[-1]], event=E(tag))
        e_rd = OP("vector", lambda v: _last([
            _last([v.tensor_reduce(stt[:, 16 + i:17 + i],
                                   pwide[:, i * 128:(i + 1) * 128], AX, AL.max),
                   v.tensor_reduce(stt[:, 20 + i:21 + i],
                                   pwide[:, 512 + i * 128:512 + (i + 1) * 128],
                                   AX, AL.min, negate=True)])
            for i in range(4)]), waits=[e_tr], event=E(tag))
        e_s2 = OP("vector", lambda v: _last([
            v.tensor_tensor(stt[:, 16:20], stt[:, 16:20], stt[:, 20:24], AL.max),
            v.tensor_scalar(stt[:, 16:20], stt[:, 16:20], 1e-5, None, AL.max),
            v.reciprocal(stt[:, 20:21], stt[:, 16:17]),
            v.reciprocal(stt[:, 21:22], stt[:, 17:18]),
            v.reciprocal(stt[:, 22:23], stt[:, 18:19]),
            v.reciprocal(stt[:, 23:24], stt[:, 19:20]),
            v.tensor_scalar(dq2sv[:, 0:4], stt[:, 16:20], 0.5 / 127.0, None, AL.mult),
            v.tensor_scalar(dq2sv[:, 0:4], dq2sv[:, 0:4], tsc_t[:, 3:4], None, AL.mult),
        ]), waits=[e_rd], event=E(tag))
        e_t2 = OP("tensor", lambda t: t.transpose(
            pwide[0:4, 0:128], stt[:, 20:24], ident[:, :]), waits=[e_s2], event=E(tag))
        e_t2e = OP("vector", lambda v: v.tensor_copy(scr2[0:4, 0:128],
                                                     pwide[0:4, 0:128]),
                   waits=[e_t2], event=E(tag))
        e_t2s = OP("gpsimd", lambda gp: gp.dma_start(
            out=dr["s2row"][:, :], in_=scr2[0:4, 0:128]),
            waits=[e_t2e], event=E(tag), dma=True, stream="s2s")
        e_bcls = []
        for i in range(4):
            e_bcls.append(OP("sync", lambda s, i=i: s.dma_start(
                out=hmin[:, i * 128:(i + 1) * 128],
                in_=dr["s2row"][i:i + 1, :].broadcast_to([128, 128])),
                waits=[e_t2s], event=E(tag), dma=True, stream="s2b"))
        e_bce = OP("vector", lambda v: v.tensor_scalar(
            hmin[:, 0:512], hmin[:, 0:512], 127.0, None, AL.mult),
            waits=e_bcls, event=E(tag))
        q2e = []
        for c in range(NHC):
            e_y = OP("vector", lambda v, c=c: v.tensor_tensor(
                scr2[:, 0:G], hid_t[:, c * G:(c + 1) * G], hmin[:, 0:G], AL.mult),
                waits=[e_bce] + ([q2e[-1]] if q2e else []), event=E(tag))
            e_q = OP("vector", lambda v, c=c: v.tensor_scalar(
                hid_t[:, c * G:(c + 1) * G], scr2[:, 0:G], MAGIC, MAGIC,
                AL.add, AL.subtract), waits=[e_y], event=E(tag))
            q2e.append(e_q)

        oevs = []
        for ti in range(4):
            qevs = []
            for quar in range(4):
                e_l2 = OP("sync", lambda s, quar=quar: s.dma_start(
                    out=w2str[:, 0:8 * D].rearrange("p (k n) -> p k n", n=D),
                    in_=dr["wq2_s"][quar * 1024:(quar + 1) * 1024, :].rearrange(
                        "(k p) n -> p k n", p=128)),
                    waits=[e_wprep, shared_w[0], q2e[-1] if ti == 0 else oevs[-1]] +
                          ([qevs[-1]] if qevs else []),
                    event=E(tag), dma=True, stream="w2q")
                e_mq = OP("tensor", lambda t, ti=ti, quar=quar: _last([
                    _last([t.matmul(pwide[:, 0:512],
                                    hid_t[:, (quar * 8 + k) * G + ti * 128:
                                          (quar * 8 + k) * G + ti * 128 + 128],
                                    w2str[:, k * D:k * D + 512],
                                    start=(quar == 0 and k == 0),
                                    stop=(quar == 3 and k == 7)),
                           t.matmul(pwide[:, 512:1024],
                                    hid_t[:, (quar * 8 + k) * G + ti * 128:
                                          (quar * 8 + k) * G + ti * 128 + 128],
                                    w2str[:, k * D + 512:k * D + 1024],
                                    start=(quar == 0 and k == 0),
                                    stop=(quar == 3 and k == 7))])
                    for k in range(8)]), waits=[e_l2, q2e[-1]], event=E(tag))
                qevs.append(e_mq)
                shared_w[0] = e_mq
            oevs.append(out_write(ti, pwide, [qevs[-1]]))
        return oevs


    q1t = {}
    q1edge = None
    ffn1_done = {}
    proj_done = {}
    st_upd = [e_bcev]
    u_loads = {}
    xq_grp_evs = {g: [] for g in range(NG)}

    def load_u(g):
        buf = xg[g % 2]
        dep = ffn1_done.get(g - 2)
        dep2 = proj_done.get(g - 2)
        return OP("sync", lambda s, g=g, buf=buf: s.dma_start(
            out=buf[:, :].rearrange("p (a b) -> p a b", a=4),
            in_=dr["u_pad"][1 + g * G:1 + (g + 1) * G, :].rearrange(
                "(a p) b -> p a b", p=128)),
            waits=([dep[-1]] if dep else []) + ([dep2[-1]] if dep2 else []),
            event=E(f"ul{g}"), dma=True, stream=f"u{g % 2}")

    prev_ffn_tail = None
    for g in range(NG + 1):
        if g < NG:
            e_ul = load_u(g)
            u_loads[g] = e_ul
            for ti in range(4):
                dep = [e_ul] + ([q1t[(g, ti - 1)]] if ti > 0 else
                                ([q1t[(g - 1, 3)]] if g > 0 else [e_bcev, e_wprep]))
                if ti == 0 and (g - 2) in ffn1_done:
                    dep = dep + [ffn1_done[g - 2][-1]]
                e_q = quant_tile(f"q1_{g}_{ti}", 128,
                                 xg[g % 2][:, ti * D:(ti + 1) * D],
                                 xqt[:, 0:D], dep)
                e_te = transpose_to_fm(f"t1_{g}_{ti}", 128, xqt[:, 0:D],
                                       xq_fm[g % 2], 1 + ti * 128, G + 2, [e_q])
                q1t[(g, ti)] = e_te
                xq_grp_evs[g].append(e_te)
            if g > 0:
                e_h = OP("vector", lambda v, g=g: v.tensor_copy(
                    xq_fm[(g - 1) % 2].rearrange("p (k w) -> p k w", w=G + 2)[:, :, 513:514],
                    xq_fm[g % 2].rearrange("p (k w) -> p k w", w=G + 2)[:, :, 1:2]),
                    waits=[q1t[(g, 0)]], event=E(f"qh{g}"))
                e_h0 = OP("vector", lambda v, g=g: v.tensor_copy(
                    xq_fm[g % 2].rearrange("p (k w) -> p k w", w=G + 2)[:, :, 0:1],
                    xq_fm[(g - 1) % 2].rearrange("p (k w) -> p k w", w=G + 2)[:, :, 512:513]),
                    waits=[q1t[(g, 0)], q1t[(g - 1, 3)]], event=E(f"qh0{g}"))
                xq_grp_evs[g - 1].append(e_h)
                xq_grp_evs[g].append(e_h0)
            if g == 0:
                e_eq = quant_edge("q1eL", dr["u_pad"][0:1, :],
                                  [e_bcev, e_wprep])
                e_te = transpose_to_fm("t1eL", 1, ebf[0:1, 0:D], xq_fm[0], 0,
                                       G + 2, [e_eq], pr_free=[q1t[(0, 0)]])
                xq_grp_evs[0].append(e_te)
        else:
            e_eq = quant_edge("q1eR", dr["u_pad"][T + 1:T + 2, :],
                              [q1t[(NG - 1, 3)]])
            OP("gpsimd", lambda g: g.dma_start(out=dr["dbg2"][:, 0:16],
                                               in_=estt[:, 0:16]),
               waits=[e_eq], event=E(), dma=True, stream="dbg")
            OP("gpsimd", lambda g: g.dma_start(out=dr["dbg2"][0:1, 16:24],
                                               in_=rowsA[0:1, 0:8]),
               waits=[e_eq], event=E(), dma=True, stream="dbg")
            e_te = transpose_to_fm("t1eR", 1, ebf[0:1, 0:D], xq_fm[(NG - 1) % 2],
                                   513, G + 2, [e_eq])
            xq_grp_evs[NG - 1].append(e_te)

        if g >= 1:
            gg = g - 1
            buf = xg[gg % 2]

            def mk_out(gg=gg, buf=buf):
                def out_write(ti, pw, deps):
                    e = OP("vector", lambda v, ti=ti: _last([
                        v.scalar_tensor_tensor(buf[:, ti * D:ti * D + 512],
                                               pw[:, 0:512], dq2sv[:, ti:ti + 1],
                                               buf[:, ti * D:ti * D + 512],
                                               AL.mult, AL.add),
                        v.scalar_tensor_tensor(buf[:, ti * D + 512:(ti + 1) * D],
                                               pw[:, 512:1024], dq2sv[:, ti:ti + 1],
                                               buf[:, ti * D + 512:(ti + 1) * D],
                                               AL.mult, AL.add),
                    ]), waits=deps, event=E(f"f1o{gg}{ti}"))
                    e_s = OP("gpsimd", lambda gp, ti=ti: gp.dma_start(
                        out=dr["x1_s"][gg * G + ti * 128:gg * G + (ti + 1) * 128, :],
                        in_=buf[:, ti * D:(ti + 1) * D]),
                        waits=[e], event=E(f"x1s{gg}{ti}"), dma=True, stream=f"x1s{gg % 2}")
                    return e_s
                return out_write

            oevs = ffn1_done[gg] = ffn_group(
                f"F1_{gg}", xq_fm[gg % 2], xq_grp_evs[gg], mk_out(),
                prev_class_evs=[prev_ffn_tail] if prev_ffn_tail else [])
            prev_ffn_tail = oevs[-1]

            pevs = []
            for ti in range(4):
                gt_i = gg * 4 + ti
                xe = oevs[ti]
                e1 = OP("scalar", lambda s, ti=ti, buf=buf: s.activation(
                    z1[:, 0:D], buf[:, ti * D:(ti + 1) * D], AF.Copy, bias=0.0,
                    scale=1.0 / D, accum_out=stt[:, 24:25]), waits=[xe], event=E())
                e2 = OP("scalar", lambda s, ti=ti, buf=buf: s.activation(
                    z1[:, 0:D], buf[:, ti * D:(ti + 1) * D], AF.Square, bias=0.0,
                    scale=1.0, accum_out=stt[:, 25:26]), waits=[xe, e1], event=E())
                e3 = OP("vector", lambda v: _last([
                    v.tensor_tensor(stt[:, 26:27], stt[:, 24:25], stt[:, 24:25], AL.mult),
                    v.scalar_tensor_tensor(stt[:, 27:28], stt[:, 25:26], 1.0 / D,
                                           stt[:, 26:27], AL.mult, AL.subtract),
                ]), waits=[e1, e2], event=E())
                e4 = OP("scalar", lambda s: s.activation(
                    stt[:, 28:29], stt[:, 27:28], AF.Sqrt, bias=flags_t[:, 2:3], scale=1.0),
                    waits=[e3], event=E())
                e5 = OP("vector", lambda v, gt_i=gt_i: _last([
                    v.reciprocal(statsv[:, gt_i * 2:gt_i * 2 + 1], stt[:, 28:29]),
                    v.scalar_tensor_tensor(statsv[:, gt_i * 2 + 1:gt_i * 2 + 2],
                                           stt[:, 24:25], -1.0,
                                           statsv[:, gt_i * 2:gt_i * 2 + 1],
                                           AL.mult, AL.mult),
                ]), waits=[e4], event=E())
                e_h = OP("vector", lambda v, ti=ti, gt_i=gt_i, buf=buf: v.tensor_scalar(
                    h_tm[:, 0:D], buf[:, ti * D:(ti + 1) * D],
                    statsv[:, gt_i * 2:gt_i * 2 + 1],
                    statsv[:, gt_i * 2 + 1:gt_i * 2 + 2], AL.mult, AL.add),
                    waits=[e5], event=E())
                e_hb = OP("vector", lambda v: v.tensor_copy(xqt[:, 0:D], h_tm[:, 0:D]),
                          waits=[e_h] + ([last_pbT[0]] if last_pbT[0] else []), event=E())
                if gg == 0 and ti == 0:
                    OP("gpsimd", lambda g: g.dma_start(out=dr["dbg3"][:, :],
                                                       in_=h_tm[:, 0:D]),
                       waits=[e_h], event=E(), dma=True, stream="dbg")
                    OP("gpsimd", lambda g: g.dma_start(out=dr["dbg2"][:, 16:18],
                                                       in_=statsv[:, 0:2]),
                       waits=[e_h], event=E(), dma=True, stream="dbg")
                e_hf = transpose_to_fm(f"hf{gt_i}", 128, xqt[:, 0:D], h_fm, 0, 128,
                                       [e_hb])
                projs = [("wv_s", "rv_s", 0), ("wmv_s", "mv_s", 1),
                         ("wq_s", "rq_s", 2), ("wmq_s", "mq_s", 3),
                         ("wh_s", "pre_s", 4)]
                tile_evs = []
                for pi, (wsn, osn, bi) in enumerate(projs):
                    e_lw = OP("sync", lambda s, wsn=wsn: s.dma_start(
                        out=w2str[:, 0:KD * D].rearrange("p (k n) -> p k n", n=D),
                        in_=dr[wsn][:, :].rearrange("(k p) n -> p k n", p=128)),
                        waits=[e_wprep, oevs[-1], shared_w[0]] +
                              (tile_evs[-1:] if tile_evs else
                               (pevs[-1:] if pevs else [])),
                        event=E(), dma=True, stream="pjw")
                    e_pm = OP("tensor", lambda t, bi=bi: _last([
                        _last([t.matmul(pb[3][:, 0:512],
                                        h_fm[:, k * 128:(k + 1) * 128],
                                        w2str[:, k * D:k * D + 512],
                                        start=(k == 0), stop=False),
                               t.matmul(pb[4][:, 0:512],
                                        h_fm[:, k * 128:(k + 1) * 128],
                                        w2str[:, k * D + 512:(k + 1) * D],
                                        start=(k == 0), stop=False)])
                        for k in range(KD)] + [
                        t.matmul(pb[3][:, 0:512], sel_b[:, bi * 128:(bi + 1) * 128],
                                 bias_tb[0:8, 0:512], start=False, stop=True),
                        t.matmul(pb[4][:, 0:512], sel_b[:, bi * 128:(bi + 1) * 128],
                                 bias_tb[0:8, 512:1024], start=False, stop=True),
                    ]), waits=[e_lw, e_hf], event=E())
                    shared_w[0] = e_pm
                    e_pe = OP("vector", lambda v: _last([
                        v.tensor_copy(projb[:, 0:512], pb[3][:, 0:512]),
                        v.tensor_copy(projb[:, 512:1024], pb[4][:, 0:512]),
                    ]), waits=[e_pm], event=E())
                    e_ps = OP("gpsimd", lambda gp, osn=osn, gt_i=gt_i: gp.dma_start(
                        out=dr[osn][gt_i * 128:(gt_i + 1) * 128, :],
                        in_=projb[:, 0:D]), waits=[e_pe], event=E(), dma=True, stream="pjs")
                    tile_evs.append(e_ps)
                    if pi == 0:
                        e_rvk = OP("vector", lambda v, ti=ti: v.tensor_copy(
                            rv_tm[:, (ti % 2) * D:(ti % 2 + 1) * D], projb[:, 0:D]),
                            waits=[e_pe, st_upd[-1]], event=E())
                        tile_evs.append(e_rvk)
                    if pi == 1:
                        e_mvk = OP("vector", lambda v, ti=ti: v.tensor_copy(
                            mv_tm[:, (ti % 2) * D:(ti % 2 + 1) * D], projb[:, 0:D]),
                            waits=[e_pe, st_upd[-1]], event=E())
                        tile_evs.append(e_mvk)
                e_gm = OP("tensor", lambda t: _last([
                    t.matmul(pb[3][:, 0:26], h_fm[:, k * 128:(k + 1) * 128],
                             gt_t[:, k * 26:(k + 1) * 26],
                             start=(k == 0), stop=(k == KD - 1))
                    for k in range(KD)]),
                    waits=[e_hf, tile_evs[-1]] + st_upd[-1:], event=E())
                e_gz = OP("vector", lambda v: v.tensor_tensor(
                    gsb[:, 0:26], pb[3][:, 0:26], gb_b[:, 0:26], AL.add),
                    waits=[e_gm], event=E())
                e_sg = OP("scalar", lambda s: _last([
                    s.activation(gsb[:, 26:27], gsb[:, 0:1], AF.Sigmoid,
                                 bias=0.0, scale=5.0),
                    s.activation(gsb[:, 27:28], gsb[:, 9:10], AF.Sigmoid,
                                 bias=0.0, scale=5.0),
                ]), waits=[e_gz], event=E())
                e_sm1 = OP("vector", lambda v: _last([
                    v.tensor_reduce(stt[:, 29:30], gsb[:, 1:9], AX, AL.max),
                    v.tensor_scalar(stt[:, 29:30], stt[:, 29:30], -1.0, None, AL.mult),
                    v.tensor_reduce(stt[:, 30:31], gsb[:, 10:26], AX, AL.max),
                    v.tensor_scalar(stt[:, 30:31], stt[:, 30:31], -1.0, None, AL.mult),
                ]), waits=[e_gz], event=E())
                e_ex = OP("scalar", lambda s: _last([
                    s.activation(gsb[:, 1:9], gsb[:, 1:9], AF.Exp,
                                 bias=stt[:, 29:30], scale=1.0,
                                 accum_out=stt[:, 31:32]),
                    s.activation(gsb[:, 10:26], gsb[:, 10:26], AF.Exp,
                                 bias=stt[:, 30:31], scale=1.0,
                                 accum_out=stt[:, 28:29]),
                ]), waits=[e_sm1, e_sg], event=E())
                e_cf = OP("vector", lambda v, gt_i=gt_i: _last([
                    v.reciprocal(stt[:, 29:30], stt[:, 31:32]),
                    v.reciprocal(stt[:, 30:31], stt[:, 28:29]),
                    v.tensor_tensor(stt[:, 29:30], stt[:, 29:30], gsb[:, 26:27], AL.mult),
                    v.tensor_tensor(stt[:, 30:31], stt[:, 30:31], gsb[:, 27:28], AL.mult),
                    v.tensor_scalar(coeff_r[:, gt_i * 8:(gt_i + 1) * 8], gsb[:, 1:9],
                                    stt[:, 29:30], None, AL.mult),
                    v.tensor_scalar(coeff_m[:, gt_i * 16:(gt_i + 1) * 16], gsb[:, 10:26],
                                    stt[:, 30:31], None, AL.mult),
                    v.tensor_scalar(cmd_b[:, 0:16],
                                    coeff_m[:, gt_i * 16:(gt_i + 1) * 16],
                                    decC_t[:, 0:1], None, AL.mult),
                ]), waits=[e_ex], event=E())
                e_dm = OP("tensor", lambda t, ti=ti, gt_i=gt_i: _last([
                    t.matmul(pb[3][0:8, 0:512], coeff_r[:, gt_i * 8:(gt_i + 1) * 8],
                             rv_tm[:, (ti % 2) * D:(ti % 2) * D + 512],
                             start=True, stop=True),
                    t.matmul(pb[4][0:8, 0:512], coeff_r[:, gt_i * 8:(gt_i + 1) * 8],
                             rv_tm[:, (ti % 2) * D + 512:(ti % 2 + 1) * D],
                             start=True, stop=True),
                    t.matmul(pb[3][32:48, 0:512], cmd_b[:, 0:16],
                             mv_tm[:, (ti % 2) * D:(ti % 2) * D + 512],
                             start=True, stop=True),
                    t.matmul(pb[4][32:48, 0:512], cmd_b[:, 0:16],
                             mv_tm[:, (ti % 2) * D + 512:(ti % 2 + 1) * D],
                             start=True, stop=True),
                ]), waits=[e_cf, e_gm, e_rvk, e_mvk] + st_upd[-1:], event=E())
                e_su = OP("vector", lambda v: _last([
                    v.tensor_tensor(st_r[0:8, 0:512], st_r[0:8, 0:512],
                                    pb[3][0:8, 0:512], AL.add),
                    v.tensor_tensor(st_r[0:8, 512:1024], st_r[0:8, 512:1024],
                                    pb[4][0:8, 0:512], AL.add),
                    v.scalar_tensor_tensor(st_m[0:16, 0:512], st_m[0:16, 0:512],
                                           DC, pb[3][32:48, 0:512], AL.mult, AL.add),
                    v.scalar_tensor_tensor(st_m[0:16, 512:1024], st_m[0:16, 512:1024],
                                           DC, pb[4][32:48, 0:512], AL.mult, AL.add),
                ]), waits=[e_dm], event=E())
                st_upd.append(e_su)
                pevs.append(e_su)
            proj_done[gg] = pevs

    e_rz = OP("vector", lambda v: _last([
        v.memset(st_r[:, :], 0.0),
        v.memset(st_m[:, :], 0.0),
    ]), waits=[st_upd[-1]], event=E("erz"))
    prevq = e_rz
    for qk in range(T // C):
        e_qlr = OP("sync", lambda s, qk=qk: s.dma_start(
            out=rv_tm[:, (qk % 2) * D:(qk % 2 + 1) * D],
            in_=dr["rv_s"][qk * C:(qk + 1) * C, :]),
            waits=[prevq], event=E(), dma=True, stream=f"erc{qk % 2}")
        e_qlm = OP("sync", lambda s, qk=qk: s.dma_start(
            out=mv_tm[:, (qk % 2) * D:(qk % 2 + 1) * D],
            in_=dr["mv_s"][qk * C:(qk + 1) * C, :]),
            waits=[e_qlr], event=E(), dma=True, stream=f"erc{qk % 2}")
        e_qc = OP("vector", lambda v, qk=qk: v.tensor_scalar(
            cmd_b[:, 0:16], coeff_m[:, qk * 16:(qk + 1) * 16],
            decC_t[:, 0:1], None, AL.mult), waits=[e_qlm, prevq], event=E())
        e_qm = OP("tensor", lambda t, qk=qk: _last([
            t.matmul(pb[0][0:8, 0:512], coeff_r[:, qk * 8:(qk + 1) * 8],
                     rv_tm[:, (qk % 2) * D:(qk % 2) * D + 512],
                     start=True, stop=True),
            t.matmul(pb[1][0:8, 0:512], coeff_r[:, qk * 8:(qk + 1) * 8],
                     rv_tm[:, (qk % 2) * D + 512:(qk % 2 + 1) * D],
                     start=True, stop=True),
            t.matmul(pb[0][32:48, 0:512], cmd_b[:, 0:16],
                     mv_tm[:, (qk % 2) * D:(qk % 2) * D + 512],
                     start=True, stop=True),
            t.matmul(pb[1][32:48, 0:512], cmd_b[:, 0:16],
                     mv_tm[:, (qk % 2) * D + 512:(qk % 2 + 1) * D],
                     start=True, stop=True),
        ]), waits=[e_qc, e_qlm, prevq], event=E())
        prevq = OP("vector", lambda v: _last([
            v.tensor_tensor(st_r[0:8, 0:512], st_r[0:8, 0:512],
                            pb[0][0:8, 0:512], AL.add),
            v.tensor_tensor(st_r[0:8, 512:1024], st_r[0:8, 512:1024],
                            pb[1][0:8, 0:512], AL.add),
            v.scalar_tensor_tensor(st_m[0:16, 0:512], st_m[0:16, 0:512],
                                   DC, pb[0][32:48, 0:512], AL.mult, AL.add),
            v.scalar_tensor_tensor(st_m[0:16, 512:1024], st_m[0:16, 512:1024],
                                   DC, pb[1][32:48, 0:512], AL.mult, AL.add),
        ]), waits=[e_qm], event=E())
    e_msk = OP("vector", lambda v: v.tensor_scalar(
        qy[0:8, 0:D], st_r[0:8, :], flags_t[0:8, 0:1], None, AL.mult),
        waits=[prevq], event=E("c1m"))
    e_ci0 = OP("gpsimd", lambda g: g.dma_start(out=dr["cc1_in"][0:8, :],
                                               in_=qy[0:8, 0:D]),
               waits=[e_msk], event=E("c1i0"), dma=True, stream="cc")
    e_msk2 = OP("vector", lambda v: v.tensor_scalar(
        qy[0:16, 0:D], st_m[0:16, :], flags_t[0:16, 0:1], None, AL.mult),
        waits=[e_ci0], event=E("c1m2"))
    e_ci = OP("gpsimd", lambda g: g.dma_start(out=dr["cc1_in"][8:24, :],
                                              in_=qy[0:16, 0:D]),
              waits=[e_msk2], event=E("c1i"), dma=True, stream="cc")
    e_cc = OP("gpsimd", lambda g: g.collective_compute(
        "AllReduce", AL.add, ins=[dr["cc1_in"][:, :]], outs=[dr["cc1_out"][:, :]],
        replica_groups=[[0, 1], [2, 3], [4, 5], [6, 7]]),
        waits=[e_ci], event=E("c1c"))
    e_co0 = OP("sync", lambda s: s.dma_start(out=qy[0:8, 0:D],
                                             in_=dr["cc1_out"][0:8, :]),
               waits=[e_cc], event=E("c1o0"), dma=True, stream="ccl")
    e_st00 = OP("vector", lambda v: _last([
        v.tensor_scalar(st_r[0:8, :], qy[0:8, 0:D], flags_t[0:8, 1:2], None, AL.mult),
        v.tensor_copy(st_rb[0:8, :], st_r[0:8, :]),
    ]), waits=[e_co0], event=E("c1s0"))
    e_co = OP("sync", lambda s: s.dma_start(out=qy[0:16, 0:D],
                                            in_=dr["cc1_out"][8:24, :]),
              waits=[e_st00], event=E("c1o"), dma=True, stream="ccl")
    e_st0 = OP("vector", lambda v: _last([
        v.tensor_scalar(st_m[0:16, :], qy[0:16, 0:D], flags_t[0:16, 1:2],
                        None, AL.mult),
        v.tensor_copy(st_mb[0:16, :], st_m[0:16, :]),
    ]), waits=[e_co], event=E("c1s"))
    e_stf = OP("tensor", lambda t: _last(
        [t.transpose(pb[3][:, k * 24:k * 24 + 8], st_r[0:8, k * 128:(k + 1) * 128],
                     ident[0:8, 0:8]) for k in range(KD)] +
        [t.transpose(pb[3][:, k * 24 + 8:k * 24 + 24],
                     st_m[0:16, k * 128:(k + 1) * 128],
                     ident[0:16, 0:16]) for k in range(KD)]),
        waits=[e_st0, e_st00], event=E("c1t"))
    e_stf2 = OP("vector", lambda v: _last([
        v.tensor_copy(st_fm[:, :], pb[3][:, 0:KD * 24]),
        v.tensor_copy(st_fmb[:, :], pb[3][:, 0:KD * 24]),
    ]), waits=[e_stf], event=E("c1e"))
    e_dbg3a = OP("gpsimd", lambda g: g.dma_start(out=dr["dbg3"][0:8, :],
                                                 in_=st_r[0:8, :]),
                 waits=[e_stf2], event=E(), dma=True, stream="dbg")
    e_dbg3b = OP("gpsimd", lambda g: g.dma_start(out=dr["dbg3"][8:24, :],
                                                 in_=st_m[0:16, :]),
                 waits=[e_stf2], event=E(), dma=True, stream="dbg")
    e_dbg3 = OP("vector", lambda v: v.engine_nop(), waits=[e_dbg3a, e_dbg3b],
                event=E())
    sweep_state = [e_dbg3]

    x2_done = {}
    q2t = {}
    xq2_grp_evs = {gq: [] for gq in range(NG)}
    ffn2_out = {}
    ffn2_tail = [prev_ffn_tail]
    ln_x_add = {}
    p2_loads = {}

    def p2_load(g):
        deps = [x2_done[g - 2][-1]] if g - 2 in x2_done else []
        if g - 2 in ffn2_out:
            deps = deps + [ffn2_out[g - 2][-1]]
        if g < 2:
            deps = deps + [proj_done[NG - 1][-1], st_upd[-1]]
        evs = {}
        evs["x1"] = OP("sync", lambda s, g=g: s.dma_start(
            out=xg[g % 2][:, :].rearrange("p (a b) -> p a b", a=4),
            in_=dr["x1_s"][g * G:(g + 1) * G, :].rearrange("(a p) b -> p a b", p=128)),
            waits=deps + [ffn1_done[g][-1]], event=E(f"p2x{g}"), dma=True, stream=f"px{g % 2}")
        evs["proj"] = proj_done[g][-1]
        for nm, bufm in [("rv_s", rv_fmt), ("mv_s", mv_fmt),
                         ("rq_s", rq_fmt), ("mq_s", mq_fmt)]:
            lastn = None
            for k in range(KD):
                lastn = OP("scalar", lambda s, nm=nm, bufm=bufm, k=k, g=g:
                           s.dma_start_transpose(
                    bufm[:, k * G:(k + 1) * G],
                    dr[nm][g * G:(g + 1) * G, k * 128:(k + 1) * 128]),
                    waits=[evs["x1"], evs["proj"]] +
                          ([sweep_done[g - 1][-1]] if g >= 1 else [])
                    if k == 0 and nm == "rv_s" else [lastn2],
                    event=E(), dma=True, stream="fmt")
                lastn2 = lastn
            evs[nm + "f"] = lastn
        evs["last"] = lastn
        return evs

    lastn2 = None
    sweep_done = {}
    out_stores = []

    def emit_ffn2(gg):
        buf = xg[gg % 2]

        def out_write(ti, pw, deps):
            if out_stores:
                deps = list(deps) + [out_stores[-1]]
            e = OP("vector", lambda v, ti=ti: _last([
                v.scalar_tensor_tensor(z1[:, 0:512], pw[:, 0:512],
                                       dq2sv[:, ti:ti + 1],
                                       buf[:, ti * D:ti * D + 512], AL.mult, AL.add),
                v.scalar_tensor_tensor(z1[:, 512:1024], pw[:, 512:1024],
                                       dq2sv[:, ti:ti + 1],
                                       buf[:, ti * D + 512:(ti + 1) * D],
                                       AL.mult, AL.add),
            ]), waits=deps, event=E(f"f2o{gg}{ti}"))
            e_s = OP("gpsimd", lambda gp, ti=ti, gg=gg: gp.dma_start(
                out=dr["out"][gg * G + ti * 128:gg * G + (ti + 1) * 128, :],
                in_=z1[:, 0:D]), waits=[e], event=E(f"os{gg}{ti}"), dma=True, stream="outs")
            out_stores.append(e_s)
            return e_s
        oevs = ffn_group(f"F2_{gg}", xq_fm[gg % 2], xq2_grp_evs[gg], out_write,
                         prev_class_evs=ffn2_tail[-1:])
        ffn2_tail.append(oevs[-1])
        return oevs

    for g in range(NG):
        ld = p2_load(g)
        p2_loads[g] = ld
        tile_out_evs = []
        for k in range(NCH):
            gt_i = g * 4 + k
            cs = k * 128
            dep_ld = [ld["last"], ld["x1"]]
            hw = (k % 2) * D
            e_lrv = OP("sync", lambda s, k=k, g=g: s.dma_start(
                out=rv_tm[:, (k % 2) * D:(k % 2 + 1) * D],
                in_=dr["rv_s"][g * G + k * 128:g * G + (k + 1) * 128, :]),
                waits=[ld["x1"], ld["proj"]] + sweep_state[-1:], event=E(), dma=True, stream="rvl")
            e_lmv = OP("sync", lambda s, k=k, g=g: s.dma_start(
                out=mv_tm[:, (k % 2) * D:(k % 2 + 1) * D],
                in_=dr["mv_s"][g * G + k * 128:g * G + (k + 1) * 128, :]),
                waits=[e_lrv], event=E(), dma=True, stream="mvl")
            e_lpre = OP("sync", lambda s, k=k, g=g: s.dma_start(
                out=pre_t[:, (k % 2) * D:(k % 2 + 1) * D],
                in_=dr["pre_s"][g * G + k * 128:g * G + (k + 1) * 128, :]),
                waits=[e_lmv], event=E(), dma=True, stream="prel")
            e_h = OP("vector", lambda v, k=k, gt_i=gt_i, g=g: v.tensor_scalar(
                h_tm[:, 0:D], xg[g % 2][:, k * D:(k + 1) * D],
                statsv[:, gt_i * 2:gt_i * 2 + 1],
                statsv[:, gt_i * 2 + 1:gt_i * 2 + 2], AL.mult, AL.add),
                waits=dep_ld + sweep_state[-1:], event=E())
            e_ct = OP("tensor", lambda t, gt_i=gt_i: _last([
                t.transpose(pbT[0:8, 512:640], coeff_r[:, gt_i * 8:(gt_i + 1) * 8],
                            identb[:, :]),
                t.transpose(pbT[0:16, 640:768], coeff_m[:, gt_i * 16:(gt_i + 1) * 16],
                            identb[:, :]),
            ]), waits=dep_ld + sweep_state[-1:] +
                ([last_pbT[0]] if last_pbT[0] else []), event=E())
            e_ce = OP("vector", lambda v: _last([
                v.tensor_copy(crf[0:8, :], pbT[0:8, 512:640]),
                v.tensor_copy(cmf[0:16, :], pbT[0:16, 640:768]),
            ]), waits=[e_ct], event=E())
            last_pbT[0] = e_ce
            e_am = OP("tensor", lambda t, cs=cs: _last(
                [t.matmul(pb[0][:, 0:128], rv_fmt[:, kd * G + cs:kd * G + cs + 128],
                          rq_fmt[:, kd * G + cs:kd * G + cs + 128],
                          start=(kd == 0), stop=(kd == KD - 1)) for kd in range(KD)] +
                [t.matmul(pb[0][:, 128:256], mv_fmt[:, kd * G + cs:kd * G + cs + 128],
                          mq_fmt[:, kd * G + cs:kd * G + cs + 128],
                          start=(kd == 0), stop=(kd == KD - 1)) for kd in range(KD)]),
                waits=[e_ce, ld["last"]], event=E())
            e_ae = OP("vector", lambda v: _last([
                v.tensor_tensor(pool2k[:, 512:640], pb[0][:, 0:128], minc_t[:, :], AL.mult),
                v.tensor_tensor(pool2k[:, 640:768], pb[0][:, 128:256], mdec_t[:, :],
                                AL.mult),
            ]), waits=[e_am], event=E())
            e_sm = OP("tensor", lambda t, cs=cs, gt_i=gt_i: _last(
                [t.matmul(pb[1][:, 0:8], rq_fmt[:, kd * G + cs:kd * G + cs + 128],
                          st_fmb[:, kd * 24:kd * 24 + 8],
                          start=(kd == 0), stop=False) for kd in range(KD)] +
                [t.matmul(pb[1][:, 8:24], mq_fmt[:, kd * G + cs:kd * G + cs + 128],
                          st_fmb[:, kd * 24 + 8:kd * 24 + 24],
                          start=False, stop=False) for kd in range(KD)] +
                [t.matmul(pb[1][:, 0:8], pool2k[:, 512:640],
                          coeff_r[:, gt_i * 8:(gt_i + 1) * 8], start=False, stop=False),
                 t.matmul(pb[1][:, 24:40], pool2k[:, 640:768],
                          coeff_m[:, gt_i * 16:(gt_i + 1) * 16], start=False, stop=True)]),
                waits=[e_ae] + sweep_state[-1:], event=E())
            e_scm = OP("vector", lambda v: _last([
                v.tensor_copy(sc_m[:, 0:16], pb[1][:, 8:24]),
                v.scalar_tensor_tensor(sc_m[:, 0:16], sc_m[:, 0:16], dec1_t[:, 0:1],
                                       pb[1][:, 24:40], AL.mult, AL.add),
            ]), waits=[e_sm], event=E())
            e_mx = OP("vector", lambda v: _last([
                v.tensor_reduce(stt[:, 24:25], pb[1][:, 0:8], AX, AL.max),
                v.tensor_scalar(stt[:, 24:25], stt[:, 24:25], -SCL, None, AL.mult),
                v.tensor_reduce(stt[:, 25:26], sc_m[:, 0:16], AX, AL.max),
                v.tensor_scalar(stt[:, 25:26], stt[:, 25:26], -SCL, None, AL.mult),
            ]), waits=[e_scm], event=E())
            e_ex = OP("scalar", lambda s: _last([
                s.activation(p_tm[:, 0:8], pb[1][:, 0:8], AF.Exp,
                             bias=stt[:, 24:25], scale=SCL, accum_out=stt[:, 26:27]),
                s.activation(p_tm[:, 8:24], sc_m[:, 0:16], AF.Exp,
                             bias=stt[:, 25:26], scale=SCL, accum_out=stt[:, 27:28]),
            ]), waits=[e_mx], event=E())
            e_pn = OP("vector", lambda v: _last([
                v.reciprocal(stt[:, 26:27], stt[:, 26:27]),
                v.reciprocal(stt[:, 27:28], stt[:, 27:28]),
                v.tensor_scalar(p_tm[:, 0:8], p_tm[:, 0:8], stt[:, 26:27], None, AL.mult),
                v.tensor_scalar(p_tm[:, 8:24], p_tm[:, 8:24], stt[:, 27:28],
                                None, AL.mult),
                v.tensor_scalar(pms[:, 0:16], p_tm[:, 8:24], dec1_t[:, 0:1],
                                None, AL.mult),
            ]), waits=[e_ex], event=E())
            e_pc = OP("vector", lambda v: _last([
                v.tensor_copy(pbf[:, 0:24], p_tm[:, 0:24]),
                v.tensor_copy(pmsbf[:, 0:16], pms[:, 0:16]),
            ]), waits=[e_pn], event=E())
            e_pt = OP("tensor", lambda t: _last([
                t.transpose(pbT[0:24, 512:640], pbf[:, 0:24], identb[:, :]),
                t.transpose(pbT[0:16, 640:768], pmsbf[:, 0:16], identb[:, :]),
                t.transpose(pbT[0:16, 768:896], pbf[:, 8:24], identb[:, :]),
            ]), waits=[e_pc] + ([last_pbT[0]] if last_pbT[0] else []), event=E())
            e_pe = OP("vector", lambda v: _last([
                v.tensor_copy(p_fm[0:24, :], pbT[0:24, 512:640]),
                v.tensor_copy(pms_fm[0:16, :], pbT[0:16, 640:768]),
                v.tensor_copy(pmr_fm[0:16, :], pbT[0:16, 768:896]),
            ]), waits=[e_pt], event=E())
            last_pbT[0] = e_pe
            e_wm = OP("tensor", lambda t, gt_i=gt_i: _last([
                t.matmul(pb[1][:, 128:256], crf[0:8, :], p_fm[0:8, :],
                         start=True, stop=True),
                t.matmul(pb[1][:, 256:384], cmf[0:16, :], pmr_fm[0:16, :],
                         start=True, stop=True),
            ]), waits=[e_pe, e_ce], event=E())
            e_we = OP("vector", lambda v: _last([
                v.tensor_tensor(pool2k[:, 768:896], pb[1][:, 128:256], minc_t[:, :], AL.mult),
                v.tensor_tensor(pool2k[:, 896:1024], pb[1][:, 256:384], mdec_t[:, :],
                                AL.mult),
            ]), waits=[e_wm], event=E())
            e_prev = [e_we]
            e_cmm = []
            for wv in range(4):
                jlist = list(range(wv * 4, wv * 4 + 4))
                e_rm = OP("tensor", lambda t, jlist=jlist, k=k, hw=hw: _last([
                    _last([
                        t.matmul(pb[2][:, (jc % 4) * 128:(jc % 4) * 128 + 128],
                                 st_rb[0:8, (jc % 8) * 128:(jc % 8) * 128 + 128]
                                 if jc < 8 else
                                 st_mb[0:16, (jc % 8) * 128:(jc % 8) * 128 + 128],
                                 p_fm[0:8, :] if jc < 8 else pms_fm[0:16, :],
                                 start=True, stop=False),
                        t.matmul(pb[2][:, (jc % 4) * 128:(jc % 4) * 128 + 128],
                                 rv_tm[:, hw + (jc % 8) * 128:
                                       hw + (jc % 8) * 128 + 128]
                                 if jc < 8 else
                                 mv_tm[:, hw + (jc % 8) * 128:
                                       hw + (jc % 8) * 128 + 128],
                                 pool2k[:, 768:896] if jc < 8 else pool2k[:, 896:1024],
                                 start=False, stop=True)])
                    for jc in jlist]), waits=e_prev + [e_lrv, e_lmv], event=E())
                e_re = OP("vector", lambda v: v.tensor_copy(
                    reads_sb[:, 0:512], pb[2][:, 0:512]), waits=[e_rm], event=E())
                e_prev = [e_re]
                e_lw = OP("sync", lambda s, wv=wv: s.dma_start(
                    out=w2str[:, (wv % 2) * 4 * D:(wv % 2) * 4 * D + 4 * D].rearrange(
                        "p (k n) -> p k n", n=D),
                    in_=dr["wrm_s"][wv * 512:(wv + 1) * 512, :].rearrange(
                        "(k p) n -> p k n", p=128)),
                    waits=[e_wprep, shared_w[0]] +
                          ([e_cmm[-2]] if len(e_cmm) >= 2 else []),
                    event=E(), dma=True, stream=f"wrm{wv % 2}")
                e_cm = OP("tensor", lambda t, wv=wv: _last([
                    _last([t.matmul(pb[0][:, 0:512],
                                    reads_sb[:, j * 128:(j + 1) * 128],
                                    w2str[:, (wv % 2) * 4 * D + j * D:
                                          (wv % 2) * 4 * D + j * D + 512],
                                    start=(wv == 0 and j == 0), stop=False),
                           t.matmul(pb[1][:, 0:512],
                                    reads_sb[:, j * 128:(j + 1) * 128],
                                    w2str[:, (wv % 2) * 4 * D + j * D + 512:
                                          (wv % 2) * 4 * D + (j + 1) * D],
                                    start=(wv == 0 and j == 0),
                                    stop=(wv == 3 and j == 3))])
                    for j in range(4)]), waits=[e_re, e_lw], event=E())
                e_cmm.append(e_cm)
                shared_w[0] = e_cm
            e_cph = OP("vector", lambda v, k=k, hw=hw: _last([
                v.tensor_tensor(h_tm[:, 0:D], h_tm[:, 0:D],
                                pre_t[:, hw:hw + D], AL.add),
                v.tensor_tensor(h_tm[:, 0:512], h_tm[:, 0:512],
                                pb[0][:, 0:512], AL.add),
                v.tensor_tensor(h_tm[:, 512:1024], h_tm[:, 512:1024],
                                pb[1][:, 0:512], AL.add),
            ]), waits=[e_cmm[-1], e_h, e_lpre], event=E())
            e_l1 = OP("scalar", lambda s: _last([
                s.activation(pb[2][:, 0:512], h_tm[:, 0:512], AF.Copy, bias=0.0,
                             scale=1.0 / D, accum_out=stt[:, 24:25]),
                s.activation(pb[2][:, 0:512], h_tm[:, 512:1024], AF.Copy, bias=0.0,
                             scale=1.0 / D, accum_out=stt[:, 31:32]),
            ]), waits=[e_cph], event=E())
            e_l2 = OP("scalar", lambda s: _last([
                s.activation(pb[2][:, 0:512], h_tm[:, 0:512], AF.Square, bias=0.0,
                             scale=1.0, accum_out=stt[:, 25:26]),
                s.activation(pb[2][:, 0:512], h_tm[:, 512:1024], AF.Square, bias=0.0,
                             scale=1.0, accum_out=stt[:, 30:31]),
            ]), waits=[e_cph, e_l1], event=E())
            e_l3 = OP("vector", lambda v: _last([
                v.tensor_add(stt[:, 24:25], stt[:, 24:25], stt[:, 31:32]),
                v.tensor_add(stt[:, 25:26], stt[:, 25:26], stt[:, 30:31]),
                v.tensor_tensor(stt[:, 26:27], stt[:, 24:25], stt[:, 24:25], AL.mult),
                v.scalar_tensor_tensor(stt[:, 27:28], stt[:, 25:26], 1.0 / D,
                                       stt[:, 26:27], AL.mult, AL.subtract),
            ]), waits=[e_l1, e_l2], event=E())
            e_l4 = OP("scalar", lambda s: s.activation(
                stt[:, 28:29], stt[:, 27:28], AF.Sqrt, bias=flags_t[:, 2:3], scale=1.0),
                waits=[e_l3], event=E())
            e_l5 = OP("vector", lambda v: _last([
                v.reciprocal(stt[:, 29:30], stt[:, 28:29]),
                v.scalar_tensor_tensor(stt[:, 30:31], stt[:, 24:25], -1.0,
                                       stt[:, 29:30], AL.mult, AL.mult),
            ]), waits=[e_l4], event=E())
            e_z1 = OP("vector", lambda v: v.tensor_scalar(
                z1[:, 0:D], h_tm[:, 0:D], stt[:, 29:30], stt[:, 30:31],
                AL.mult, AL.add), waits=[e_l5], event=E())
            e_z2 = OP("vector", lambda v: v.tensor_tensor(
                z1[:, 0:D], z1[:, 0:D], lng_b[:, 0:D], AL.mult),
                waits=[e_z1], event=E())
            e_z3 = OP("vector", lambda v: v.tensor_tensor(
                z1[:, 0:D], z1[:, 0:D], lnb_b[:, 0:D], AL.add),
                waits=[e_z2], event=E())
            e_x2 = OP("vector", lambda v, k=k, g=g: v.tensor_tensor(
                xg[g % 2][:, k * D:(k + 1) * D], xg[g % 2][:, k * D:(k + 1) * D],
                z1[:, 0:D], AL.add), waits=[e_z3, e_h], event=E())
            tile_out_evs.append(e_x2)
            e_cd = OP("vector", lambda v, gt_i=gt_i: _last([
                v.tensor_scalar(cmd_b[:, 0:16], coeff_m[:, gt_i * 16:(gt_i + 1) * 16],
                                decC_t[:, 0:1], None, AL.mult),
            ]), waits=[e_ce], event=E())
            e_dn = OP("tensor", lambda t, k=k, gt_i=gt_i, hw=hw: _last([
                t.matmul(pb[2][0:8, 0:512], coeff_r[:, gt_i * 8:(gt_i + 1) * 8],
                         rv_tm[:, hw:hw + 512], start=True, stop=True),
                t.matmul(pb[0][0:8, 0:512], coeff_r[:, gt_i * 8:(gt_i + 1) * 8],
                         rv_tm[:, hw + 512:hw + D], start=True, stop=True),
                t.matmul(pb[2][32:48, 0:512], cmd_b[:, 0:16],
                         mv_tm[:, hw:hw + 512], start=True, stop=True),
                t.matmul(pb[0][32:48, 0:512], cmd_b[:, 0:16],
                         mv_tm[:, hw + 512:hw + D], start=True, stop=True),
            ]), waits=[e_cd, e_cph, e_x2], event=E())
            e_df = OP("tensor", lambda t, k=k, hw=hw, g=g: _last([
                t.matmul(pb[1][:, 128 + kd * 24:128 + kd * 24 + 8],
                         rv_tm[:, hw + kd * 128:hw + (kd + 1) * 128],
                         coeff_r[:, (g * 4 + k) * 8:(g * 4 + k + 1) * 8],
                         start=True, stop=True) for kd in range(KD)] + [
                t.matmul(pb[1][:, 128 + kd * 24 + 8:128 + kd * 24 + 24],
                         mv_tm[:, hw + kd * 128:hw + (kd + 1) * 128],
                         cmd_b[:, 0:16], start=True, stop=True) for kd in range(KD)]),
                waits=[e_dn], event=E())
            e_sun = OP("vector", lambda v: _last([
                v.tensor_tensor(st_r[0:8, 0:512], st_r[0:8, 0:512],
                                pb[2][0:8, 0:512], AL.add),
                v.tensor_tensor(st_r[0:8, 512:1024], st_r[0:8, 512:1024],
                                pb[0][0:8, 0:512], AL.add),
                v.scalar_tensor_tensor(st_m[0:16, 0:512], st_m[0:16, 0:512], DC,
                                       pb[2][32:48, 0:512], AL.mult, AL.add),
                v.scalar_tensor_tensor(st_m[0:16, 512:1024], st_m[0:16, 512:1024],
                                       DC, pb[0][32:48, 0:512], AL.mult, AL.add),
                v.tensor_copy(st_rb[0:8, :], st_r[0:8, :]),
                v.tensor_copy(st_mb[0:16, :], st_m[0:16, :]),
            ]), waits=[e_df], event=E())
            e_suf = OP("vector", lambda v: _last([
                v.tensor_tensor(
                    st_fm[:, :].rearrange("p (kd n) -> p kd n", n=24)[:, :, 0:8],
                    st_fm[:, :].rearrange("p (kd n) -> p kd n", n=24)[:, :, 0:8],
                    pb[1][:, 128:320].rearrange("p (kd n) -> p kd n", n=24)[:, :, 0:8],
                    AL.add),
                v.scalar_tensor_tensor(
                    st_fm[:, :].rearrange("p (kd n) -> p kd n", n=24)[:, :, 8:24],
                    st_fm[:, :].rearrange("p (kd n) -> p kd n", n=24)[:, :, 8:24],
                    DC,
                    pb[1][:, 128:320].rearrange("p (kd n) -> p kd n", n=24)[:, :, 8:24],
                    AL.mult, AL.add),
                v.tensor_copy(st_fmb[:, :], st_fm[:, :]),
            ]), waits=[e_df, e_sun], event=E())
            sweep_state.append(e_suf)
        sweep_done[g] = tile_out_evs

        if g == 0:
            e_sva0 = OP("gpsimd", lambda gp: gp.dma_start(
                out=dr["cc2_out"][0:1, :], in_=xg[0][0:1, 0:D]),
                waits=[tile_out_evs[0]], event=E("sv0a"), dma=True, stream="cc")
            e_sva1 = OP("sync", lambda s: s.dma_start(
                out=rowsA[32:33, 0:D], in_=dr["cc2_out"][0:1, :]),
                waits=[e_sva0], event=E("sv0b"), dma=True, stream="ccl")
            e_sva2 = OP("vector", lambda v: v.tensor_scalar(
                rowsA[32:33, 0:D], rowsA[32:33, 0:D], flags_t[32:33, 1:2],
                None, AL.mult), waits=[e_sva1], event=E("sv0c"))
            e_sva = OP("gpsimd", lambda gp: gp.dma_start(
                out=dr["cc2_in"][1:2, :], in_=rowsA[32:33, 0:D]),
                waits=[e_sva2], event=E("sv0"), dma=True, stream="cc")
        if g == NG - 1:
            e_svb0 = OP("gpsimd", lambda gp: gp.dma_start(
                out=dr["cc2_out"][0:1, :], in_=xg[(NG - 1) % 2][127:128, 3 * D:4 * D]),
                waits=[tile_out_evs[3], e_sva], event=E("sv1a"), dma=True, stream="cc")
            e_svb1 = OP("sync", lambda s: s.dma_start(
                out=rowsA[32:33, 0:D], in_=dr["cc2_out"][0:1, :]),
                waits=[e_svb0, e_sva2], event=E("sv1b"), dma=True, stream="ccl")
            e_svb2 = OP("vector", lambda v: v.tensor_scalar(
                rowsA[32:33, 0:D], rowsA[32:33, 0:D], flags_t[32:33, 0:1],
                None, AL.mult), waits=[e_svb1], event=E("sv1c"))
            e_svb = OP("gpsimd", lambda gp: gp.dma_start(
                out=dr["cc2_in"][0:1, :], in_=rowsA[32:33, 0:D]),
                waits=[e_svb2], event=E("sv1"), dma=True, stream="cc")

        for ti in range(4):
            dep = [tile_out_evs[ti]] + ([q2t[(g, ti - 1)]] if ti > 0 else
                                        ([q2t[(g - 1, 3)]] if g > 0 else
                                         [xq_grp_evs[NG - 1][-1]]))
            if ti == 0 and (g - 2) in ffn2_out:
                dep = dep + [ffn2_out[g - 2][-1]]
            e_q = quant_tile(f"q2_{g}_{ti}", 128, xg[g % 2][:, ti * D:(ti + 1) * D],
                             xqt[:, 0:D], dep)
            e_te = transpose_to_fm(f"t2_{g}_{ti}", 128, xqt[:, 0:D],
                                   xq_fm[g % 2], 1 + ti * 128, G + 2, [e_q])
            q2t[(g, ti)] = e_te
            xq2_grp_evs[g].append(e_te)
        x2_done[g] = list(tile_out_evs)
        if g > 0:
            e_h0 = OP("vector", lambda v, g=g: v.tensor_copy(
                xq_fm[g % 2].rearrange("p (kd w) -> p kd w", w=G + 2)[:, :, 0:1],
                xq_fm[(g - 1) % 2].rearrange("p (kd w) -> p kd w", w=G + 2)[:, :, 512:513]),
                waits=[q2t[(g, 0)], q2t[(g - 1, 3)]], event=E(f"2h0{g}"))
            e_h1 = OP("vector", lambda v, g=g: v.tensor_copy(
                xq_fm[(g - 1) % 2].rearrange("p (kd w) -> p kd w", w=G + 2)[:, :, 513:514],
                xq_fm[g % 2].rearrange("p (kd w) -> p kd w", w=G + 2)[:, :, 1:2]),
                waits=[q2t[(g, 0)]], event=E(f"2h1{g}"))
            xq2_grp_evs[g].append(e_h0)
            xq2_grp_evs[g - 1].append(e_h1)
        if g == 0:
            pass

        if g >= 1:
            gg = g - 1
            ffn2_out[gg] = emit_ffn2(gg)

    e_cc2 = OP("gpsimd", lambda gp: gp.collective_compute(
        "AllReduce", AL.add, ins=[dr["cc2_in"][:, :]], outs=[dr["cc2_out"][:, :]],
        replica_groups=[[0, 1], [2, 3], [4, 5], [6, 7]]),
        waits=[e_sva, e_svb], event=E("c2c"))
    e_leq = quant_edge("q2eL", dr["cc2_out"][0:1, :], [e_cc2], mask_col=1)
    e_let = transpose_to_fm("t2eL", 1, ebf[0:1, 0:D], xq_fm[0], 0, G + 2, [e_leq],
                            pr_free=[q2t[(1, 0)]] if NG > 1 else [])
    xq2_grp_evs[0].append(e_let)
    e_req = quant_edge("q2eR", dr["cc2_out"][1:2, :], [e_cc2, e_let], mask_col=0)
    e_ret = transpose_to_fm("t2eR", 1, ebf[0:1, 0:D], xq_fm[(NG - 1) % 2], 513,
                            G + 2, [e_req])
    xq2_grp_evs[NG - 1].append(e_ret)

    ffn2_out[NG - 1] = emit_ffn2(NG - 1)

    OP("gpsimd", lambda gp: gp.nop(), waits=[out_stores[-1]], event=E("fin"))

    block = es.enter_context(nc.Block())
    sems = {e: es.enter_context(nc.semaphore(f"s_{e}")) for e in Prog.ENGINES}
    dsems = {s: es.enter_context(nc.semaphore(f"d_{s}")) for s in P.dma_counts}
    P.emit(block, sems, dsems)
    es.close()
    return nc


def kernel(**inputs):
    i = lambda n: np.asarray(inputs[n], dtype=np.float32)
    u = i("L0_attn_u")
    B, L, _ = u.shape
    out = np.empty((B, L, D), dtype=np.float32)

    idn = np.eye(128, dtype=np.float32)
    ii = np.arange(C)
    minc = (ii[:, None] <= ii[None, :]).astype(np.float32)
    mdec = minc * (DECAY ** np.maximum(ii[None, :] - ii[:, None], 0)).astype(np.float32)
    dec1 = (DECAY ** (ii + 1)).astype(np.float32).reshape(C, 1)
    decC = (DECAY ** (C - 1 - ii)).astype(np.float32).reshape(C, 1)
    onesr = np.ones((1, 128), dtype=np.float32)
    sel = np.zeros((8, 5 * 128), dtype=np.float32)
    for bi in range(5):
        sel[bi, bi * 128:(bi + 1) * 128] = 1.0

    cw = i("L0_ffn_conv_w").reshape(HID, 3)
    cw_cols = np.zeros((128, NHC * 3), dtype=np.float32)
    cb_cols = np.zeros((128, NHC), dtype=np.float32)
    cb = i("L0_ffn_conv_b")
    for c in range(NHC):
        cw_cols[:, 3 * c:3 * c + 3] = cw[c * 128:(c + 1) * 128, :]
        cb_cols[:, c] = cb[c * 128:(c + 1) * 128]

    gT = np.concatenate([i("L0_attn_reg_gate_w"), i("L0_attn_reg_addr_w"),
                         i("L0_attn_mem_gate_w"), i("L0_attn_mem_addr_w")],
                        axis=0).T
    gTp = np.ascontiguousarray(
        gT.reshape(KD, 128, 26).transpose(1, 0, 2).reshape(128, KD * 26))
    gateb = np.zeros((1, 32), dtype=np.float32)
    gateb[0, 0] = i("L0_attn_reg_gate_b")[0]
    gateb[0, 1:9] = i("L0_attn_reg_addr_b")
    gateb[0, 9] = i("L0_attn_mem_gate_b")[0]
    gateb[0, 10:26] = i("L0_attn_mem_addr_b")

    bias_rows = np.zeros((8, D), dtype=np.float32)
    bias_rows[0] = i("L0_attn_reg_val_b")
    bias_rows[1] = i("L0_attn_mem_val_b")
    bias_rows[2] = i("L0_attn_reg_q_b")
    bias_rows[3] = i("L0_attn_mem_q_b")
    bias_rows[4] = i("L0_attn_comb_b")
    bias_rows[5] = i("L0_attn_ln_g")
    bias_rows[6] = i("L0_attn_ln_b")

    comb = i("L0_attn_comb_w")
    whT = np.ascontiguousarray(comb[:, 0:D].T)
    wrmT = np.ascontiguousarray(
        np.concatenate([comb[:, D:2 * D].T, comb[:, 2 * D:3 * D].T], axis=0))
    w1T = np.ascontiguousarray(i("L0_ffn_w1").T)
    w2T = np.ascontiguousarray(i("L0_ffn_w2").T)
    wvT = np.ascontiguousarray(i("L0_attn_reg_val_w").T)
    wmvT = np.ascontiguousarray(i("L0_attn_mem_val_w").T)
    wqT = np.ascontiguousarray(i("L0_attn_reg_q_w").T)
    wmqT = np.ascontiguousarray(i("L0_attn_mem_q_w").T)

    nc = build_program()
    in_maps = []
    for core in range(8):
        b, hh = core // 2, core % 2
        s0 = hh * T
        u_pad = np.zeros((T + 2, D), dtype=np.float32)
        u_pad[1:T + 1] = u[b, s0:s0 + T]
        if s0 > 0:
            u_pad[0] = u[b, s0 - 1]
        if s0 + T < L:
            u_pad[T + 1] = u[b, s0 + T]
        flags = np.array([[1.0 - hh, float(hh), EPS, 0.0]], dtype=np.float32)
        in_maps.append({
            "u_pad": u_pad, "w1T": w1T, "w2T": w2T,
            "cw_cols": cw_cols, "cb_cols": cb_cols,
            "wvT": wvT, "wmvT": wmvT, "wqT": wqT, "wmqT": wmqT,
            "whT": whT, "wrmT": wrmT, "gTp": gTp,
            "bias_rows": bias_rows, "gateb_row": gateb,
            "ident": idn, "identb": idn.astype(ml_dtypes.bfloat16),
            "onesr": onesr, "selb": sel.astype(ml_dtypes.bfloat16),
            "minc": minc.astype(ml_dtypes.bfloat16), "mdec": mdec.astype(ml_dtypes.bfloat16),
            "dec1": dec1, "decC": decC, "flags": flags,
        })
    res = run_bass_kernel_spmd(nc, in_maps, core_ids=list(range(8)))
    for core in range(8):
        b, hh = core // 2, core % 2
        out[b, hh * T:(hh + 1) * T, :] = res.results[core]["out"]
    return out


# version 10 (final)
# speedup vs baseline: 1.0446x; 1.0446x over previous
import os
import numpy as np
import ml_dtypes
import concourse.bass as bass
import concourse.mybir as mybir
from concourse.bass_utils import run_bass_kernel_spmd
from contextlib import ExitStack

F32 = mybir.dt.float32
BF16 = mybir.dt.bfloat16
AL = mybir.AluOpType
AF = mybir.ActivationFunctionType
AX = mybir.AxisListType.X

MAGIC = 12582912.0
DECAY = 0.995
D = 1024
HID = 4096
T = 4096
G = 512
NG = T // G
C = 128
NCH = G // C
SCL = float(D) ** -0.5
EPS = 1e-5
DC = float(DECAY ** C)
NHC = HID // 128
KD = D // 128


class Prog:
    ENGINES = ("tensor", "vector", "scalar", "gpsimd", "sync")

    def __init__(self):
        self.ops = {e: [] for e in self.ENGINES}
        self.events = {}
        self.counts = {e: 0 for e in self.ENGINES}
        self.dma_counts = {}
        self.dma_events = {}

    def op(self, eng, fn, waits=(), event=None, dma=False, stream=None):
        if isinstance(waits, str):
            waits = (waits,)
        waits = tuple(w for w in waits if w is not None)
        if dma:
            if stream is None:
                stream = "q_" + eng
            self.dma_counts[stream] = self.dma_counts.get(stream, 0) + 16
            if event is not None:
                assert event not in self.dma_events, event
                self.dma_events[event] = (stream, self.dma_counts[stream])
        elif event is not None:
            self.counts[eng] += 1
            assert event not in self.events, event
            self.events[event] = (eng, self.counts[eng])
        self.ops[eng].append((fn, waits, event, dma, stream))
        return event

    DRAIN_OPS = {"tensor_copy", "tensor_tensor", "tensor_add", "tensor_sub",
                 "tensor_mul", "tensor_max", "tensor_scalar", "tensor_single_scalar",
                 "tensor_scalar_add", "tensor_scalar_mul", "tensor_scalar_max",
                 "tensor_scalar_min", "tensor_reduce", "reduce_max", "reduce_sum",
                 "scalar_tensor_tensor", "reciprocal", "memset", "memzero",
                 "activation", "sign", "square", "sqrt", "mul", "copy", "add",
                 "select", "transpose", "bn_stats", "bn_aggr"}

    class _DrainProxy:
        def __init__(self, eng):
            self._e = eng
            self._prev_small = True

        @staticmethod
        def _op_cols(a, k):
            try:
                out = a[0] if a else k.get("out")
                shp = getattr(out, "shape", None)
                if shp is None:
                    return 0
                n = 1
                for s in list(shp)[1:]:
                    n *= int(s)
                return n
            except Exception:
                return 0

        def __getattr__(self, name):
            f = getattr(self._e, name)
            if name in Prog.DRAIN_OPS:
                eng = self._e
                proxy = self

                def wrapped(*a, **k):
                    if proxy._prev_small:
                        eng.drain()
                    proxy._prev_small = Prog._DrainProxy._op_cols(a, k) < 480
                    return f(*a, **k)
                return wrapped
            return f

    def emit(self, block, sems, dma_sems):
        for eng_name in self.ENGINES:
            ops = self.ops[eng_name]

            def make(eng_name, ops):
                def body(eng):
                    if eng_name in ("vector", "scalar"):
                        eng = Prog._DrainProxy(eng)
                    last = {}
                    for fn, waits, event, dma, stream in ops:
                        for w in waits:
                            if w in self.events:
                                e2, val = self.events[w]
                                if e2 == eng_name:
                                    continue
                                sem, key = sems[e2], (e2, 0)
                            else:
                                e2, val = self.dma_events[w]
                                sem, key = dma_sems[e2], (e2, 1)
                            if last.get(key, -1) < val:
                                eng.wait_ge(sem, val)
                                last[key] = val
                        instr = fn(eng)
                        if event is not None:
                            instr.then_inc(dma_sems[stream] if dma else sems[eng_name],
                                           16 if dma else 1)
                return body

            getattr(block, eng_name)(make(eng_name, ops))


def _last(x):
    return x[-1]


def build_program():
    nc = bass.Bass("TRN2", target_bir_lowering=False)
    P = Prog()
    dr = {}

    def din(name, shape, dtype=F32):
        dr[name] = nc.dram_tensor(name, shape, dtype, kind="ExternalInput").ap()

    def dout(name, shape, dtype=F32):
        dr[name] = nc.dram_tensor(name, shape, dtype, kind="ExternalOutput").ap()

    DBG = os.environ.get("KMEGA_DEBUG", "")
    dbg_names = set(DBG.split(",")) if DBG else set()

    def dint(name, shape, dtype=F32):
        kind = "ExternalOutput" if name in dbg_names else "Internal"
        dr[name] = nc.dram_tensor(name, shape, dtype, kind=kind).ap()

    din("u_pad", [T + 2, D])
    din("w1T", [D, HID]); din("w2T", [HID, D])
    din("cw_cols", [128, NHC * 3]); din("cb_cols", [128, NHC])
    din("wvT", [D, D]); din("wmvT", [D, D]); din("wqT", [D, D]); din("wmqT", [D, D])
    din("whT", [D, D]); din("wrmT", [2 * D, D])
    din("gTp", [128, KD * 26])
    din("bias_rows", [8, D])
    din("gateb_row", [1, 32])
    din("ident", [128, 128]); din("identb", [128, 128], BF16)
    din("onesr", [1, 128])
    din("selb", [8, 5 * 128], BF16)
    din("minc", [C, C], BF16); din("mdec", [C, C], BF16)
    din("dec1", [C, 1]); din("decC", [C, 1])
    din("flags", [1, 4])
    dout("out", [T, D])

    dint("wq1b_s", [128, KD * HID], BF16)
    dint("wq2_s", [HID, D], BF16)
    for n in ("wv_s", "wmv_s", "wq_s", "wmq_s", "wh_s"):
        dint(n, [D, D], BF16)
    dint("wrm_s", [2 * D, D], BF16)
    dint("x1_s", [T, D], F32)
    dint("pre_s", [T, D], BF16)
    for n in ("rv_s", "mv_s", "rq_s", "mq_s"):
        dint(n, [T, D], BF16)
    dint("tsc", [1, 4], F32)
    dint("s2row", [4, 128], F32)
    dint("dbg1", [128, 8], F32)
    dint("dbg2", [128, 24], F32)
    dint("dbg3", [128, D], F32)
    dint("cc1_in", [24, D], F32); dint("cc1_out", [24, D], F32)
    dint("cc2_in", [2, D], F32); dint("cc2_out", [2, D], F32)
    dint("ccb_in", [1, 16], F32); dint("ccb_out", [1, 16], F32)

    es = ExitStack()

    def sbuf(name, shape, dtype=F32):
        return es.enter_context(nc.sbuf_tensor(name, shape, dtype))

    ident = sbuf("c_id", [128, 128], F32)
    identb = sbuf("c_idb", [128, 128], BF16)
    onesr = sbuf("c_1r", [1, 128], F32)
    sel_b = sbuf("c_selb", [8, 5 * 128], BF16)
    bias_tb = sbuf("c_bib", [8, D], BF16)
    minc_t = sbuf("c_mi", [C, C], BF16)
    mdec_t = sbuf("c_md", [C, C], BF16)
    dec1_t = sbuf("c_d1", [C, 1], F32)
    decC_t = sbuf("c_dC", [C, 1], F32)
    flags_t = sbuf("c_fl", [128, 4], F32)
    bias_t = sbuf("c_bi", [8, D], F32)
    gateb_t = sbuf("c_gb", [1, 32], F32)
    gb_b = sbuf("c_gbb", [128, 32], F32)
    lng_b = sbuf("c_lng", [128, D], F32)
    lnb_b = sbuf("c_lnb", [128, D], F32)
    cw_t = sbuf("c_cw", [128, NHC * 3], F32)
    cb_t = sbuf("c_cb", [128, NHC], F32)
    gt_t = sbuf("c_gt", [128, KD * 26], BF16)
    tsc_t = sbuf("c_ts", [128, 4], F32)
    coeff_r = sbuf("r_cr", [128, (T // C) * 8], BF16)
    coeff_m = sbuf("r_cm", [128, (T // C) * 16], BF16)
    statsv = sbuf("r_sv", [128, (T // C) * 2], F32)
    st_r = sbuf("r_str", [8, D], F32)
    st_m = sbuf("r_stm", [16, D], F32)
    st_rb = sbuf("r_strb", [8, D], BF16)
    st_mb = sbuf("r_stmb", [16, D], BF16)
    st_fm = sbuf("r_sfm", [128, KD * 24], F32)
    st_fmb = sbuf("r_sfmb", [128, KD * 24], BF16)
    dq2sv = sbuf("r_dq2", [128, 4], F32)

    wstr = sbuf("w_s", [128, 2 * KD * 128], BF16)
    w2str = sbuf("w_2s", [128, 8 * D], BF16)
    xg = [sbuf(f"b_x{i}", [128, 4 * D], F32) for i in range(2)]
    h_tm = sbuf("b_h", [128, D], F32)
    pool2k = sbuf("b_p2k", [128, 1024], BF16)
    xq_fm = [sbuf(f"b_q{i}", [128, KD * (G + 2)], BF16) for i in range(2)]
    hid_t = sbuf("b_hid", [128, NHC * G], BF16)
    hmax = sbuf("b_hmx", [128, G], F32)
    hmin = sbuf("b_hmn", [128, G], F32)
    scr2 = sbuf("b_sc2", [128, G], F32)
    qy = sbuf("b_qy", [128, D], F32)
    xqt = sbuf("b_xqt", [128, D], BF16)
    stt = sbuf("b_st", [128, 32], F32)
    qstt = sbuf("b_qst", [2, 32], F32)
    rowsA = sbuf("b_rA", [128, D], F32)
    estt = sbuf("b_estt", [128, 32], F32)
    projb = sbuf("b_pb", [128, D], BF16)
    gsb = sbuf("b_gs", [128, 32], F32)
    rv_tm = sbuf("p_rv", [128, 2 * D], BF16)
    mv_tm = sbuf("p_mv", [128, 2 * D], BF16)
    rv_fmt = sbuf("p_rvf", [128, KD * G], BF16)
    mv_fmt = sbuf("p_mvf", [128, KD * G], BF16)
    rq_fmt = sbuf("p_rqf", [128, KD * G], BF16)
    mq_fmt = sbuf("p_mqf", [128, KD * G], BF16)
    pre_t = sbuf("p_pr", [128, 2 * D], BF16)
    p_tm = sbuf("p_p", [128, 24], F32)
    pms = sbuf("p_pm", [128, 16], F32)
    pbf = sbuf("p_pbf", [128, 24], BF16)
    pmsbf = sbuf("p_pmb", [128, 16], BF16)
    p_fm = sbuf("p_pf", [24, C], BF16)
    pms_fm = sbuf("p_pmf", [16, C], BF16)
    pmr_fm = sbuf("p_pmr", [16, C], BF16)
    crf = sbuf("p_crf", [8, C], BF16)
    cmf = sbuf("p_cmf", [16, C], BF16)
    cmd_b = sbuf("p_cd", [128, 16], BF16)
    sc_m = sbuf("p_sm", [128, 16], F32)
    z1 = sbuf("p_z1", [128, D], F32)
    wpart = sbuf("w_pt", [128, 44], F32)

    h_fm = pool2k
    reads_sb = pool2k
    ebf = pool2k
    a_sb_o = 512
    w_sb_o = 768
    pb = [es.enter_context(nc.psum_tensor(f"pb{i}", [128, 512], F32)) for i in range(5)]
    pwide = es.enter_context(nc.psum_tensor("pwide", [128, 1024], F32))
    pbT = es.enter_context(nc.psum_tensor("pbT", [128, 1024], BF16))

    ev_n = [0]
    shared_w = [None]

    def E(tag=""):
        ev_n[0] += 1
        return f"e{ev_n[0]}{tag}"

    OP = P.op

    cl = []
    for dst, src in [(ident, "ident"), (identb, "identb"), (onesr, "onesr"), (sel_b, "selb"),
                     (minc_t, "minc"), (mdec_t, "mdec"), (dec1_t, "dec1"),
                     (decC_t, "decC"), (bias_t, "bias_rows"), (gateb_t, "gateb_row"),
                     (cw_t, "cw_cols"), (cb_t, "cb_cols")]:
        cl.append(OP("sync", lambda s, d=dst, n=src: s.dma_start(out=d[:], in_=dr[n][:]),
                     event=E(), dma=True, stream="c"))
    cl.append(OP("sync", lambda s: s.dma_start(
        out=flags_t[:], in_=dr["flags"][0:1, :].broadcast_to([128, 4])),
        event=E(), dma=True, stream="c"))
    cl.append(OP("sync", lambda s: s.dma_start(
        out=w2str[:, 0:2 * KD * 26].bitcast(F32), in_=dr["gTp"][:, :]), event=E(), dma=True, stream="c"))
    e_gtc = OP("vector", lambda v: v.tensor_copy(gt_t[:, :], w2str[:, 0:2 * KD * 26].bitcast(F32)),
               waits=[cl[-1]], event=E())
    cl.append(OP("sync", lambda s: s.dma_start(
        out=lng_b[:, :], in_=dr["bias_rows"][5:6, :].broadcast_to([128, D])),
        event=E(), dma=True, stream="c"))
    cl.append(OP("sync", lambda s: s.dma_start(
        out=lnb_b[:, :], in_=dr["bias_rows"][6:7, :].broadcast_to([128, D])),
        event=E(), dma=True, stream="c"))
    cl.append(OP("sync", lambda s: s.dma_start(
        out=gb_b[:, :], in_=dr["gateb_row"][0:1, :].broadcast_to([128, 32])),
        event=E(), dma=True, stream="c"))
    e_btb = OP("vector", lambda v: v.tensor_copy(bias_tb[:, :], bias_t[:, :]),
               waits=cl, event=E())
    e_bcmm = OP("tensor", lambda t: t.nop(), waits=cl + [e_btb], event=E())
    e_bcev = OP("vector", lambda v: _last([
        v.memset(st_fm[:, :], 0.0),
        v.memset(st_fmb[:, :], 0.0),
        v.memset(rowsA[:, :], 0.0),
        v.memset(stt[:, :], 0.0),
    ]), waits=[e_bcmm], event=E())

    absdone = []
    for wi, (wname, rows, cols) in enumerate([("w1T", D, HID), ("w2T", HID, D)]):
        for chk in range(rows // 128):
            e_l = OP("sync", lambda s, wn=wname, ck=chk, cn=cols: s.dma_start(
                out=w2str[:, 0:2 * cn].bitcast(F32),
                in_=dr[wn][ck * 128:(ck + 1) * 128, :]),
                waits=[absdone[-1]] if absdone else [e_gtc], event=E(), dma=True, stream="prep")
            col = chk if wi == 0 else 8 + chk
            e_a = OP("scalar", lambda s, cn=cols: s.activation(
                hid_t[:, 0:2 * cn].bitcast(F32), w2str[:, 0:2 * cn].bitcast(F32),
                AF.Abs, bias=0.0, scale=1.0), waits=[e_l] + absdone[-1:], event=E())
            e_r = OP("vector", lambda v, cn=cols, col=col: v.tensor_reduce(
                wpart[:, col:col + 1], hid_t[:, 0:2 * cn].bitcast(F32), AX, AL.add),
                waits=[e_a], event=E())
            absdone.append(e_r)
    e_ones = OP("vector", lambda v: v.memset(wpart[:, 40:41], 1.0),
                waits=[e_bcev], event=E())
    e_sum = OP("vector", lambda v: _last([
        v.tensor_reduce(wpart[:, 41:42], wpart[:, 0:8], AX, AL.add),
        v.tensor_reduce(wpart[:, 42:43], wpart[:, 8:40], AX, AL.add),
    ]), waits=[absdone[-1]], event=E())
    e_tot = OP("tensor", lambda t: t.matmul(pb[0][0:2, 0:1], wpart[:, 41:43],
                                            wpart[:, 40:41], start=True, stop=True),
               waits=[e_sum, e_ones], event=E())
    e_sc = OP("vector", lambda v: v.tensor_scalar(
        qstt[0:2, 20:21], pb[0][0:2, 0:1], 1.0 / (D * HID), 1e-5,
        AL.mult, AL.max), waits=[e_tot], event=E())
    e_scw = OP("gpsimd", lambda g: g.dma_start(
        out=dr["tsc"][0:1, 2:4].rearrange("a b -> b a"), in_=qstt[0:2, 20:21]),
        waits=[e_sc], event=E(), dma=True, stream="tscw")
    e_scb0 = OP("sync", lambda s: s.dma_start(
        out=tsc_t[:], in_=dr["tsc"][0:1, :].broadcast_to([128, 4])),
        waits=[e_scw], event=E(), dma=True, stream="tscb")
    e_scb = OP("vector", lambda v: v.reciprocal(tsc_t[:, 0:2], tsc_t[:, 2:4]),
               waits=[e_scb0], event=E())

    wqdone = []
    for wi, (wname, rows, cols) in enumerate([("w1T", D, HID), ("w2T", HID, D)]):
        for chk in range(rows // 128):
            e_l = OP("sync", lambda s, wn=wname, ck=chk, cn=cols: s.dma_start(
                out=w2str[:, 0:2 * cn].bitcast(F32),
                in_=dr[wn][ck * 128:(ck + 1) * 128, :]),
                waits=[wqdone[-1]] if wqdone else [e_scb], event=E(), dma=True, stream="prep")
            e_q = OP("vector", lambda v, wi=wi, cn=cols: _last([
                v.tensor_scalar(w2str[:, 0:2 * cn].bitcast(F32),
                                w2str[:, 0:2 * cn].bitcast(F32),
                                tsc_t[:, wi:wi + 1], MAGIC, AL.mult, AL.add),
                v.tensor_scalar(w2str[:, 0:2 * cn].bitcast(F32),
                                w2str[:, 0:2 * cn].bitcast(F32),
                                MAGIC, 1.0, AL.subtract, AL.min),
                v.tensor_scalar(w2str[:, 2 * cn:3 * cn] if cn == D else hid_t[:, 0:cn],
                                w2str[:, 0:2 * cn].bitcast(F32), -1.0, None, AL.max),
            ]), waits=[e_l, e_scb], event=E())
            if wi == 0:
                for q in range(4):
                    e_s = OP("gpsimd", lambda g, ck=chk, q=q: g.dma_start(
                        out=dr["wq1b_s"].rearrange("p (c k n) -> p c k n", k=KD, n=128)[
                            :, q * 8:(q + 1) * 8, ck, :],
                        in_=hid_t[:, 0:HID].rearrange("p (c n) -> p c n", n=128)[
                            :, q * 8:(q + 1) * 8, :]),
                        waits=[e_q], event=E(), dma=True, stream="wps")
            else:
                e_s = OP("gpsimd", lambda g, ck=chk: g.dma_start(
                    out=dr["wq2_s"][ck * 128:(ck + 1) * 128, :],
                    in_=w2str[:, 2 * D:3 * D]),
                    waits=[e_q], event=E(), dma=True, stream="wps")
            wqdone.append(e_s)
    for wname, sname, rows in [("wvT", "wv_s", D), ("wmvT", "wmv_s", D),
                               ("wqT", "wq_s", D), ("wmqT", "wmq_s", D),
                               ("whT", "wh_s", D), ("wrmT", "wrm_s", 2 * D)]:
        for chk in range(rows // 128):
            e_l = OP("sync", lambda s, wn=wname, ck=chk: s.dma_start(
                out=w2str[:, 0:2 * D].bitcast(F32),
                in_=dr[wn][ck * 128:(ck + 1) * 128, :]),
                waits=[wqdone[-1]], event=E(), dma=True, stream="prep")
            e_c = OP("vector", lambda v: v.tensor_copy(
                w2str[:, 2 * D:3 * D], w2str[:, 0:2 * D].bitcast(F32)),
                waits=[e_l], event=E())
            e_s = OP("gpsimd", lambda g, sn=sname, ck=chk: g.dma_start(
                out=dr[sn][ck * 128:(ck + 1) * 128, :], in_=w2str[:, 2 * D:3 * D]),
                waits=[e_c], event=E(), dma=True, stream="wps")
            wqdone.append(e_s)
    e_wprep = wqdone[-1]
    shared_w[0] = e_wprep

    last_pbT = [None]

    def quant_tile(tag, np_, src, dst_bf, deps):
        st = stt if np_ == 128 else qstt
        buf = qy if np_ == 128 else src.tensor if False else qy
        dmp = qy if np_ == 128 else rowsA[32:33, :]
        e1 = OP("scalar", lambda s: s.activation(
            dmp[0:np_, 0:D], src, AF.Copy, bias=0.0, scale=1.0 / D,
            accum_out=st[0:np_, 0:1]), waits=deps, event=E(tag))
        e2 = OP("scalar", lambda s: s.activation(
            dmp[0:np_, 0:D], src, AF.Square, bias=0.0, scale=1.0,
            accum_out=st[0:np_, 1:2]), waits=deps + [e1], event=E(tag))
        e3 = OP("vector", lambda v: _last([
            v.tensor_reduce(st[0:np_, 2:3], src, AX, AL.max),
            v.tensor_reduce(st[0:np_, 3:4], src, AX, AL.min),
        ]), waits=deps, event=E(tag))
        e4 = OP("vector", lambda v: _last([
            v.tensor_tensor(st[0:np_, 4:5], st[0:np_, 0:1], st[0:np_, 0:1], AL.mult),
            v.scalar_tensor_tensor(st[0:np_, 5:6], st[0:np_, 1:2], 1.0 / D,
                                   st[0:np_, 4:5], AL.mult, AL.subtract),
        ]), waits=[e1, e2], event=E(tag))
        e5 = OP("scalar", lambda s: s.activation(
            st[0:np_, 6:7], st[0:np_, 5:6], AF.Sqrt, bias=flags_t[0:np_, 2:3], scale=1.0),
            waits=[e4], event=E(tag))
        e6 = OP("vector", lambda v: _last([
            v.reciprocal(st[0:np_, 7:8], st[0:np_, 6:7]),
            v.tensor_sub(st[0:np_, 8:9], st[0:np_, 2:3], st[0:np_, 0:1]),
            v.tensor_sub(st[0:np_, 9:10], st[0:np_, 0:1], st[0:np_, 3:4]),
            v.tensor_max(st[0:np_, 8:9], st[0:np_, 8:9], st[0:np_, 9:10]),
            v.tensor_tensor(st[0:np_, 10:11], st[0:np_, 8:9], st[0:np_, 7:8], AL.mult),
            v.tensor_scalar(st[0:np_, 10:11], st[0:np_, 10:11], 1e-5, None, AL.max),
            v.reciprocal(st[0:np_, 11:12], st[0:np_, 10:11]),
            v.tensor_scalar(st[0:np_, 11:12], st[0:np_, 11:12], 127.0, None, AL.mult),
            v.tensor_tensor(st[0:np_, 12:13], st[0:np_, 7:8], st[0:np_, 11:12], AL.mult),
            v.scalar_tensor_tensor(st[0:np_, 13:14], st[0:np_, 0:1], -1.0,
                                   st[0:np_, 12:13], AL.mult, AL.mult),
            v.tensor_scalar(st[0:np_, 14:15], st[0:np_, 10:11], 1.0 / 127.0,
                            None, AL.mult),
        ]), waits=[e3, e5], event=E(tag))
        e7 = OP("vector", lambda v: v.tensor_scalar(
            buf[0:np_, 0:D], src, st[0:np_, 12:13], st[0:np_, 13:14], AL.mult, AL.add),
            waits=[e6], event=E(tag))
        e8 = OP("vector", lambda v: v.tensor_scalar(
            buf[0:np_, 0:D], buf[0:np_, 0:D], MAGIC, MAGIC, AL.add, AL.subtract),
            waits=[e7], event=E(tag))
        e9 = OP("vector", lambda v: v.tensor_scalar(
            dst_bf, buf[0:np_, 0:D], st[0:np_, 14:15], None, AL.mult),
            waits=[e8] + ([last_pbT[0]] if last_pbT[0] else []), event=E(tag))
        return e9

    def quant_edge(tag, dram_row, deps, mask_col=None):
        st = estt
        e_l = OP("sync", lambda s: s.dma_start(
            out=rowsA[:, :], in_=dram_row.broadcast_to([128, D])),
            waits=deps, event=E(tag), dma=True, stream="edg")
        if mask_col is not None:
            e_l = OP("vector", lambda v: v.tensor_scalar(
                rowsA[:, :], rowsA[:, :], flags_t[:, mask_col:mask_col + 1],
                None, AL.mult), waits=[e_l], event=E(tag))
        e0 = OP("vector", lambda v: _last([
            v.tensor_reduce(st[:, 2:3], rowsA[:, :], AX, AL.max),
            v.tensor_reduce(st[:, 3:4], rowsA[:, :], AX, AL.min),
        ]), waits=[e_l], event=E(tag))
        e1 = OP("scalar", lambda s: _last([
            s.activation(rowsA[:, 0:512], rowsA[:, 0:512], AF.Copy, bias=0.0,
                         scale=1.0, accum_out=st[:, 0:1]),
            s.activation(rowsA[:, 512:1024], rowsA[:, 512:1024], AF.Copy, bias=0.0,
                         scale=1.0, accum_out=st[:, 16:17]),
            s.activation(rowsA[:, 0:512], rowsA[:, 0:512], AF.Square, bias=0.0,
                         scale=1.0, accum_out=st[:, 1:2]),
            s.activation(rowsA[:, 512:1024], rowsA[:, 512:1024], AF.Square, bias=0.0,
                         scale=1.0, accum_out=st[:, 17:18]),
        ]), waits=[e_l, e0], event=E(tag))
        e_rl = OP("sync", lambda s: s.dma_start(
            out=rowsA[:, :], in_=dram_row.broadcast_to([128, D])),
            waits=[e1], event=E(tag), dma=True, stream="edg")
        if mask_col is not None:
            e_rl = OP("vector", lambda v: v.tensor_scalar(
                rowsA[:, :], rowsA[:, :], flags_t[:, mask_col:mask_col + 1],
                None, AL.mult), waits=[e_rl], event=E(tag))
        e3 = OP("vector", lambda v: _last([
            v.tensor_add(st[:, 0:1], st[:, 0:1], st[:, 16:17]),
            v.tensor_add(st[:, 1:2], st[:, 1:2], st[:, 17:18]),
            v.tensor_scalar(st[:, 0:1], st[:, 0:1], 1.0 / D, None, AL.mult),
            v.tensor_tensor(st[:, 4:5], st[:, 0:1], st[:, 0:1], AL.mult),
            v.scalar_tensor_tensor(st[:, 5:6], st[:, 1:2], 1.0 / D,
                                   st[:, 4:5], AL.mult, AL.subtract),
        ]), waits=[e1], event=E(tag))
        e5 = OP("scalar", lambda s: s.activation(
            st[:, 6:7], st[:, 5:6], AF.Sqrt, bias=flags_t[:, 2:3], scale=1.0),
            waits=[e3], event=E(tag))
        e6 = OP("vector", lambda v: _last([
            v.reciprocal(st[:, 7:8], st[:, 6:7]),
            v.tensor_sub(st[:, 8:9], st[:, 2:3], st[:, 0:1]),
            v.tensor_sub(st[:, 9:10], st[:, 0:1], st[:, 3:4]),
            v.tensor_max(st[:, 8:9], st[:, 8:9], st[:, 9:10]),
            v.tensor_tensor(st[:, 10:11], st[:, 8:9], st[:, 7:8], AL.mult),
            v.tensor_scalar(st[:, 10:11], st[:, 10:11], 1e-5, None, AL.max),
            v.reciprocal(st[:, 11:12], st[:, 10:11]),
            v.tensor_scalar(st[:, 11:12], st[:, 11:12], 127.0, None, AL.mult),
            v.tensor_tensor(st[:, 12:13], st[:, 7:8], st[:, 11:12], AL.mult),
            v.scalar_tensor_tensor(st[:, 13:14], st[:, 0:1], -1.0,
                                   st[:, 12:13], AL.mult, AL.mult),
            v.tensor_scalar(st[:, 14:15], st[:, 10:11], 1.0 / 127.0, None, AL.mult),
        ]), waits=[e3, e5], event=E(tag))
        e7 = OP("vector", lambda v: _last([
            v.tensor_scalar(rowsA[:, :], rowsA[:, :], st[:, 12:13], st[:, 13:14],
                            AL.mult, AL.add),
            v.tensor_scalar(rowsA[:, :], rowsA[:, :], MAGIC, MAGIC,
                            AL.add, AL.subtract),
            v.tensor_scalar(ebf[0:1, 0:D], rowsA[0:1, :], st[0:1, 14:15],
                            None, AL.mult),
        ]), waits=[e6, e_rl], event=E(tag))
        return e7

    def transpose_to_fm(tag, np_, src_bf, dstbuf, coff, width, deps, pr_free=()):
        e_t = OP("tensor", lambda t: _last([
            t.transpose(pbT[:, k * 128:k * 128 + np_],
                        src_bf[0:np_, k * 128:(k + 1) * 128], identb[0:np_, 0:np_])
            for k in range(KD)]),
            waits=deps + list(pr_free) + ([last_pbT[0]] if last_pbT[0] else []),
            event=E(tag))
        e_e = OP("vector", lambda v: _last([
            v.tensor_copy(dstbuf[:, k * width + coff: k * width + coff + np_],
                          pbT[:, k * 128:k * 128 + np_]) for k in range(KD)]),
            waits=[e_t], event=E(tag))
        last_pbT[0] = e_e
        return e_e

    def ffn_group(tag, xqb, xq_deps, out_write, prev_class_evs=()):
        W = G + 2
        h_evs = []
        mm_evs = []
        ld_prev = []
        for c in range(NHC):
            boff = (c % 2) * KD * 128
            e_l = OP("sync", lambda s, c=c, boff=boff: s.dma_start(
                out=wstr[:, boff:boff + KD * 128],
                in_=dr["wq1b_s"][:, c * KD * 128:(c + 1) * KD * 128]),
                waits=([mm_evs[c - 2]] if c >= 2 else [e_wprep]) +
                      (list(prev_class_evs) if c < 2 else []),
                event=E(tag), dma=True, stream=f"w1_{c % 2}")
            e_mm = OP("tensor", lambda t, boff=boff: _last([
                _last([t.matmul(pwide[:, 0:512],
                                wstr[:, boff + k * 128:boff + (k + 1) * 128],
                                xqb[:, k * W:k * W + 512],
                                start=(k == 0), stop=(k == KD - 1)),
                       t.matmul(pwide[:, 512:514],
                                wstr[:, boff + k * 128:boff + (k + 1) * 128],
                                xqb[:, k * W + 512:k * W + 514],
                                start=(k == 0), stop=(k == KD - 1))])
                for k in range(KD)]),
                waits=[e_l] + list(xq_deps) + ([h_evs[-1][1]] if h_evs else []),
                event=E(tag))
            mm_evs.append(e_mm)
            e_c1 = OP("scalar", lambda s, c=c: s.activation(
                scr2[:, 0:512], pwide[:, 0:512], AF.Copy, bias=0.0,
                scale=cw_t[:, 3 * c:3 * c + 1]), waits=[e_mm], event=E(tag))
            e_c2 = OP("vector", lambda v, c=c: v.scalar_tensor_tensor(
                scr2[:, 0:512], pwide[:, 1:513], cw_t[:, 3 * c + 1:3 * c + 2],
                scr2[:, 0:512], AL.mult, AL.add), waits=[e_c1], event=E(tag))
            e_c3 = OP("vector", lambda v, c=c: v.scalar_tensor_tensor(
                scr2[:, 0:512], pwide[:, 2:514], cw_t[:, 3 * c + 2:3 * c + 3],
                scr2[:, 0:512], AL.mult, AL.add), waits=[e_c2], event=E(tag))
            e_si = OP("scalar", lambda s, c=c: s.activation(
                hid_t[:, c * G:(c + 1) * G], scr2[:, 0:512], AF.Silu,
                bias=cb_t[:, c:c + 1], scale=tsc_t[:, 2:3]),
                waits=[e_c3], event=E(tag))
            if c == 0:
                e_mx = OP("vector", lambda v: _last([
                    v.tensor_scalar(hmax[:, :], hid_t[:, 0:G], 1.0, None, AL.mult),
                    v.tensor_scalar(hmin[:, :], hid_t[:, 0:G], 1.0, None, AL.mult),
                ]), waits=[e_si], event=E(tag))
            else:
                e_mx = OP("vector", lambda v, c=c: _last([
                    v.tensor_tensor(hmax[:, :], hmax[:, :],
                                    hid_t[:, c * G:(c + 1) * G], AL.max),
                    v.tensor_tensor(hmin[:, :], hmin[:, :],
                                    hid_t[:, c * G:(c + 1) * G], AL.min),
                ]), waits=[e_si, h_evs[-1][0]], event=E(tag))
            h_evs.append((e_mx, e_si))

        e_tr = OP("tensor", lambda t: _last([
            _last([t.transpose(pwide[:, i * 128:(i + 1) * 128],
                               hmax[:, i * 128:(i + 1) * 128], ident[:, :]),
                   t.transpose(pwide[:, 512 + i * 128:512 + (i + 1) * 128],
                               hmin[:, i * 128:(i + 1) * 128], ident[:, :])])
            for i in range(4)]), waits=[h_evs[-1][0], mm_evs[-1]], event=E(tag))
        e_rd = OP("vector", lambda v: _last([
            _last([v.tensor_reduce(stt[:, 16 + i:17 + i],
                                   pwide[:, i * 128:(i + 1) * 128], AX, AL.max),
                   v.tensor_reduce(stt[:, 20 + i:21 + i],
                                   pwide[:, 512 + i * 128:512 + (i + 1) * 128],
                                   AX, AL.min, negate=True)])
            for i in range(4)]), waits=[e_tr], event=E(tag))
        e_s2 = OP("vector", lambda v: _last([
            v.tensor_tensor(stt[:, 16:20], stt[:, 16:20], stt[:, 20:24], AL.max),
            v.tensor_scalar(stt[:, 16:20], stt[:, 16:20], 1e-5, None, AL.max),
            v.reciprocal(stt[:, 20:21], stt[:, 16:17]),
            v.reciprocal(stt[:, 21:22], stt[:, 17:18]),
            v.reciprocal(stt[:, 22:23], stt[:, 18:19]),
            v.reciprocal(stt[:, 23:24], stt[:, 19:20]),
            v.tensor_scalar(dq2sv[:, 0:4], stt[:, 16:20], 0.5 / 127.0, None, AL.mult),
            v.tensor_scalar(dq2sv[:, 0:4], dq2sv[:, 0:4], tsc_t[:, 3:4], None, AL.mult),
        ]), waits=[e_rd], event=E(tag))
        e_t2 = OP("tensor", lambda t: t.transpose(
            pwide[0:4, 0:128], stt[:, 20:24], ident[:, :]), waits=[e_s2], event=E(tag))
        e_t2e = OP("vector", lambda v: v.tensor_copy(scr2[0:4, 0:128],
                                                     pwide[0:4, 0:128]),
                   waits=[e_t2], event=E(tag))
        e_t2s = OP("gpsimd", lambda gp: gp.dma_start(
            out=dr["s2row"][:, :], in_=scr2[0:4, 0:128]),
            waits=[e_t2e], event=E(tag), dma=True, stream="s2s")
        e_bcls = []
        for i in range(4):
            e_bcls.append(OP("sync", lambda s, i=i: s.dma_start(
                out=hmin[:, i * 128:(i + 1) * 128],
                in_=dr["s2row"][i:i + 1, :].broadcast_to([128, 128])),
                waits=[e_t2s], event=E(tag), dma=True, stream="s2b"))
        e_bce = OP("vector", lambda v: v.tensor_scalar(
            hmin[:, 0:512], hmin[:, 0:512], 127.0, None, AL.mult),
            waits=e_bcls, event=E(tag))
        q2e = []
        for c in range(NHC):
            e_y = OP("vector", lambda v, c=c: v.tensor_tensor(
                scr2[:, 0:G], hid_t[:, c * G:(c + 1) * G], hmin[:, 0:G], AL.mult),
                waits=[e_bce] + ([q2e[-1]] if q2e else []), event=E(tag))
            e_q = OP("vector", lambda v, c=c: v.tensor_scalar(
                hid_t[:, c * G:(c + 1) * G], scr2[:, 0:G], MAGIC, MAGIC,
                AL.add, AL.subtract), waits=[e_y], event=E(tag))
            q2e.append(e_q)

        oevs = []
        for ti in range(4):
            qevs = []
            for quar in range(4):
                e_l2 = OP("sync", lambda s, quar=quar: s.dma_start(
                    out=w2str[:, 0:8 * D].rearrange("p (k n) -> p k n", n=D),
                    in_=dr["wq2_s"][quar * 1024:(quar + 1) * 1024, :].rearrange(
                        "(k p) n -> p k n", p=128)),
                    waits=[e_wprep, shared_w[0], q2e[-1] if ti == 0 else oevs[-1]] +
                          ([qevs[-1]] if qevs else []),
                    event=E(tag), dma=True, stream="w2q")
                e_mq = OP("tensor", lambda t, ti=ti, quar=quar: _last([
                    _last([t.matmul(pwide[:, 0:512],
                                    hid_t[:, (quar * 8 + k) * G + ti * 128:
                                          (quar * 8 + k) * G + ti * 128 + 128],
                                    w2str[:, k * D:k * D + 512],
                                    start=(quar == 0 and k == 0),
                                    stop=(quar == 3 and k == 7)),
                           t.matmul(pwide[:, 512:1024],
                                    hid_t[:, (quar * 8 + k) * G + ti * 128:
                                          (quar * 8 + k) * G + ti * 128 + 128],
                                    w2str[:, k * D + 512:k * D + 1024],
                                    start=(quar == 0 and k == 0),
                                    stop=(quar == 3 and k == 7))])
                    for k in range(8)]), waits=[e_l2, q2e[-1]], event=E(tag))
                qevs.append(e_mq)
                shared_w[0] = e_mq
            oevs.append(out_write(ti, pwide, [qevs[-1]]))
        return oevs


    q1t = {}
    q1edge = None
    ffn1_done = {}
    proj_done = {}
    st_upd = [e_bcev]
    u_loads = {}
    xq_grp_evs = {g: [] for g in range(NG)}

    def load_u(g):
        buf = xg[g % 2]
        dep = ffn1_done.get(g - 2)
        dep2 = proj_done.get(g - 2)
        return OP("sync", lambda s, g=g, buf=buf: s.dma_start(
            out=buf[:, :].rearrange("p (a b) -> p a b", a=4),
            in_=dr["u_pad"][1 + g * G:1 + (g + 1) * G, :].rearrange(
                "(a p) b -> p a b", p=128)),
            waits=([dep[-1]] if dep else []) + ([dep2[-1]] if dep2 else []),
            event=E(f"ul{g}"), dma=True, stream=f"u{g % 2}")

    prev_ffn_tail = None
    for g in range(NG + 1):
        if g < NG:
            e_ul = load_u(g)
            u_loads[g] = e_ul
            for ti in range(4):
                dep = [e_ul] + ([q1t[(g, ti - 1)]] if ti > 0 else
                                ([q1t[(g - 1, 3)]] if g > 0 else [e_bcev, e_wprep]))
                if ti == 0 and (g - 2) in ffn1_done:
                    dep = dep + [ffn1_done[g - 2][-1]]
                e_q = quant_tile(f"q1_{g}_{ti}", 128,
                                 xg[g % 2][:, ti * D:(ti + 1) * D],
                                 xqt[:, 0:D], dep)
                e_te = transpose_to_fm(f"t1_{g}_{ti}", 128, xqt[:, 0:D],
                                       xq_fm[g % 2], 1 + ti * 128, G + 2, [e_q])
                q1t[(g, ti)] = e_te
                xq_grp_evs[g].append(e_te)
            if g > 0:
                e_h = OP("vector", lambda v, g=g: v.tensor_copy(
                    xq_fm[(g - 1) % 2].rearrange("p (k w) -> p k w", w=G + 2)[:, :, 513:514],
                    xq_fm[g % 2].rearrange("p (k w) -> p k w", w=G + 2)[:, :, 1:2]),
                    waits=[q1t[(g, 0)]], event=E(f"qh{g}"))
                e_h0 = OP("vector", lambda v, g=g: v.tensor_copy(
                    xq_fm[g % 2].rearrange("p (k w) -> p k w", w=G + 2)[:, :, 0:1],
                    xq_fm[(g - 1) % 2].rearrange("p (k w) -> p k w", w=G + 2)[:, :, 512:513]),
                    waits=[q1t[(g, 0)], q1t[(g - 1, 3)]], event=E(f"qh0{g}"))
                xq_grp_evs[g - 1].append(e_h)
                xq_grp_evs[g].append(e_h0)
            if g == 0:
                e_eq = quant_edge("q1eL", dr["u_pad"][0:1, :],
                                  [e_bcev, e_wprep])
                e_te = transpose_to_fm("t1eL", 1, ebf[0:1, 0:D], xq_fm[0], 0,
                                       G + 2, [e_eq], pr_free=[q1t[(0, 0)]])
                xq_grp_evs[0].append(e_te)
        else:
            e_eq = quant_edge("q1eR", dr["u_pad"][T + 1:T + 2, :],
                              [q1t[(NG - 1, 3)]])
            OP("gpsimd", lambda g: g.dma_start(out=dr["dbg2"][:, 0:16],
                                               in_=estt[:, 0:16]),
               waits=[e_eq], event=E(), dma=True, stream="dbg")
            OP("gpsimd", lambda g: g.dma_start(out=dr["dbg2"][0:1, 16:24],
                                               in_=rowsA[0:1, 0:8]),
               waits=[e_eq], event=E(), dma=True, stream="dbg")
            e_te = transpose_to_fm("t1eR", 1, ebf[0:1, 0:D], xq_fm[(NG - 1) % 2],
                                   513, G + 2, [e_eq])
            xq_grp_evs[NG - 1].append(e_te)

        if g >= 1:
            gg = g - 1
            buf = xg[gg % 2]

            def mk_out(gg=gg, buf=buf):
                def out_write(ti, pw, deps):
                    e = OP("vector", lambda v, ti=ti: _last([
                        v.scalar_tensor_tensor(buf[:, ti * D:ti * D + 512],
                                               pw[:, 0:512], dq2sv[:, ti:ti + 1],
                                               buf[:, ti * D:ti * D + 512],
                                               AL.mult, AL.add),
                        v.scalar_tensor_tensor(buf[:, ti * D + 512:(ti + 1) * D],
                                               pw[:, 512:1024], dq2sv[:, ti:ti + 1],
                                               buf[:, ti * D + 512:(ti + 1) * D],
                                               AL.mult, AL.add),
                    ]), waits=deps, event=E(f"f1o{gg}{ti}"))
                    e_s = OP("gpsimd", lambda gp, ti=ti: gp.dma_start(
                        out=dr["x1_s"][gg * G + ti * 128:gg * G + (ti + 1) * 128, :],
                        in_=buf[:, ti * D:(ti + 1) * D]),
                        waits=[e], event=E(f"x1s{gg}{ti}"), dma=True, stream=f"x1s{gg % 2}")
                    return e_s
                return out_write

            oevs = ffn1_done[gg] = ffn_group(
                f"F1_{gg}", xq_fm[gg % 2], xq_grp_evs[gg], mk_out(),
                prev_class_evs=[prev_ffn_tail] if prev_ffn_tail else [])
            prev_ffn_tail = oevs[-1]

            pevs = []
            for ti in range(4):
                gt_i = gg * 4 + ti
                xe = oevs[ti]
                e1 = OP("scalar", lambda s, ti=ti, buf=buf: s.activation(
                    z1[:, 0:D], buf[:, ti * D:(ti + 1) * D], AF.Copy, bias=0.0,
                    scale=1.0 / D, accum_out=stt[:, 24:25]), waits=[xe], event=E())
                e2 = OP("scalar", lambda s, ti=ti, buf=buf: s.activation(
                    z1[:, 0:D], buf[:, ti * D:(ti + 1) * D], AF.Square, bias=0.0,
                    scale=1.0, accum_out=stt[:, 25:26]), waits=[xe, e1], event=E())
                e3 = OP("vector", lambda v: _last([
                    v.tensor_tensor(stt[:, 26:27], stt[:, 24:25], stt[:, 24:25], AL.mult),
                    v.scalar_tensor_tensor(stt[:, 27:28], stt[:, 25:26], 1.0 / D,
                                           stt[:, 26:27], AL.mult, AL.subtract),
                ]), waits=[e1, e2], event=E())
                e4 = OP("scalar", lambda s: s.activation(
                    stt[:, 28:29], stt[:, 27:28], AF.Sqrt, bias=flags_t[:, 2:3], scale=1.0),
                    waits=[e3], event=E())
                e5 = OP("vector", lambda v, gt_i=gt_i: _last([
                    v.reciprocal(statsv[:, gt_i * 2:gt_i * 2 + 1], stt[:, 28:29]),
                    v.scalar_tensor_tensor(statsv[:, gt_i * 2 + 1:gt_i * 2 + 2],
                                           stt[:, 24:25], -1.0,
                                           statsv[:, gt_i * 2:gt_i * 2 + 1],
                                           AL.mult, AL.mult),
                ]), waits=[e4], event=E())
                e_h = OP("vector", lambda v, ti=ti, gt_i=gt_i, buf=buf: v.tensor_scalar(
                    h_tm[:, 0:D], buf[:, ti * D:(ti + 1) * D],
                    statsv[:, gt_i * 2:gt_i * 2 + 1],
                    statsv[:, gt_i * 2 + 1:gt_i * 2 + 2], AL.mult, AL.add),
                    waits=[e5], event=E())
                e_hb = OP("vector", lambda v: v.tensor_copy(xqt[:, 0:D], h_tm[:, 0:D]),
                          waits=[e_h] + ([last_pbT[0]] if last_pbT[0] else []), event=E())
                if gg == 0 and ti == 0:
                    OP("gpsimd", lambda g: g.dma_start(out=dr["dbg3"][:, :],
                                                       in_=h_tm[:, 0:D]),
                       waits=[e_h], event=E(), dma=True, stream="dbg")
                    OP("gpsimd", lambda g: g.dma_start(out=dr["dbg2"][:, 16:18],
                                                       in_=statsv[:, 0:2]),
                       waits=[e_h], event=E(), dma=True, stream="dbg")
                e_hf = transpose_to_fm(f"hf{gt_i}", 128, xqt[:, 0:D], h_fm, 0, 128,
                                       [e_hb])
                projs = [("wv_s", "rv_s", 0), ("wmv_s", "mv_s", 1),
                         ("wq_s", "rq_s", 2), ("wmq_s", "mq_s", 3),
                         ("wh_s", "pre_s", 4)]
                tile_evs = []
                for pi, (wsn, osn, bi) in enumerate(projs):
                    e_lw = OP("sync", lambda s, wsn=wsn: s.dma_start(
                        out=w2str[:, 0:KD * D].rearrange("p (k n) -> p k n", n=D),
                        in_=dr[wsn][:, :].rearrange("(k p) n -> p k n", p=128)),
                        waits=[e_wprep, oevs[-1], shared_w[0]] +
                              (tile_evs[-1:] if tile_evs else
                               (pevs[-1:] if pevs else [])),
                        event=E(), dma=True, stream="pjw")
                    e_pm = OP("tensor", lambda t, bi=bi: _last([
                        _last([t.matmul(pb[3][:, 0:512],
                                        h_fm[:, k * 128:(k + 1) * 128],
                                        w2str[:, k * D:k * D + 512],
                                        start=(k == 0), stop=False),
                               t.matmul(pb[4][:, 0:512],
                                        h_fm[:, k * 128:(k + 1) * 128],
                                        w2str[:, k * D + 512:(k + 1) * D],
                                        start=(k == 0), stop=False)])
                        for k in range(KD)] + [
                        t.matmul(pb[3][:, 0:512], sel_b[:, bi * 128:(bi + 1) * 128],
                                 bias_tb[0:8, 0:512], start=False, stop=True),
                        t.matmul(pb[4][:, 0:512], sel_b[:, bi * 128:(bi + 1) * 128],
                                 bias_tb[0:8, 512:1024], start=False, stop=True),
                    ]), waits=[e_lw, e_hf], event=E())
                    shared_w[0] = e_pm
                    e_pe = OP("vector", lambda v: _last([
                        v.tensor_copy(projb[:, 0:512], pb[3][:, 0:512]),
                        v.tensor_copy(projb[:, 512:1024], pb[4][:, 0:512]),
                    ]), waits=[e_pm], event=E())
                    e_ps = OP("gpsimd", lambda gp, osn=osn, gt_i=gt_i: gp.dma_start(
                        out=dr[osn][gt_i * 128:(gt_i + 1) * 128, :],
                        in_=projb[:, 0:D]), waits=[e_pe], event=E(), dma=True, stream="pjs")
                    tile_evs.append(e_ps)
                    if pi == 0:
                        e_rvk = OP("vector", lambda v, ti=ti: v.tensor_copy(
                            rv_tm[:, (ti % 2) * D:(ti % 2 + 1) * D], projb[:, 0:D]),
                            waits=[e_pe, st_upd[-1]], event=E())
                        tile_evs.append(e_rvk)
                    if pi == 1:
                        e_mvk = OP("vector", lambda v, ti=ti: v.tensor_copy(
                            mv_tm[:, (ti % 2) * D:(ti % 2 + 1) * D], projb[:, 0:D]),
                            waits=[e_pe, st_upd[-1]], event=E())
                        tile_evs.append(e_mvk)
                e_gm = OP("tensor", lambda t: _last([
                    t.matmul(pb[3][:, 0:26], h_fm[:, k * 128:(k + 1) * 128],
                             gt_t[:, k * 26:(k + 1) * 26],
                             start=(k == 0), stop=(k == KD - 1))
                    for k in range(KD)]),
                    waits=[e_hf, tile_evs[-1]] + st_upd[-1:], event=E())
                e_gz = OP("vector", lambda v: v.tensor_tensor(
                    gsb[:, 0:26], pb[3][:, 0:26], gb_b[:, 0:26], AL.add),
                    waits=[e_gm], event=E())
                e_sg = OP("scalar", lambda s: _last([
                    s.activation(gsb[:, 26:27], gsb[:, 0:1], AF.Sigmoid,
                                 bias=0.0, scale=5.0),
                    s.activation(gsb[:, 27:28], gsb[:, 9:10], AF.Sigmoid,
                                 bias=0.0, scale=5.0),
                ]), waits=[e_gz], event=E())
                e_sm1 = OP("vector", lambda v: _last([
                    v.tensor_reduce(stt[:, 29:30], gsb[:, 1:9], AX, AL.max),
                    v.tensor_scalar(stt[:, 29:30], stt[:, 29:30], -1.0, None, AL.mult),
                    v.tensor_reduce(stt[:, 30:31], gsb[:, 10:26], AX, AL.max),
                    v.tensor_scalar(stt[:, 30:31], stt[:, 30:31], -1.0, None, AL.mult),
                ]), waits=[e_gz], event=E())
                e_ex = OP("scalar", lambda s: _last([
                    s.activation(gsb[:, 1:9], gsb[:, 1:9], AF.Exp,
                                 bias=stt[:, 29:30], scale=1.0,
                                 accum_out=stt[:, 31:32]),
                    s.activation(gsb[:, 10:26], gsb[:, 10:26], AF.Exp,
                                 bias=stt[:, 30:31], scale=1.0,
                                 accum_out=stt[:, 28:29]),
                ]), waits=[e_sm1, e_sg], event=E())
                e_cf = OP("vector", lambda v, gt_i=gt_i: _last([
                    v.reciprocal(stt[:, 29:30], stt[:, 31:32]),
                    v.reciprocal(stt[:, 30:31], stt[:, 28:29]),
                    v.tensor_tensor(stt[:, 29:30], stt[:, 29:30], gsb[:, 26:27], AL.mult),
                    v.tensor_tensor(stt[:, 30:31], stt[:, 30:31], gsb[:, 27:28], AL.mult),
                    v.tensor_scalar(coeff_r[:, gt_i * 8:(gt_i + 1) * 8], gsb[:, 1:9],
                                    stt[:, 29:30], None, AL.mult),
                    v.tensor_scalar(coeff_m[:, gt_i * 16:(gt_i + 1) * 16], gsb[:, 10:26],
                                    stt[:, 30:31], None, AL.mult),
                    v.tensor_scalar(cmd_b[:, 0:16],
                                    coeff_m[:, gt_i * 16:(gt_i + 1) * 16],
                                    decC_t[:, 0:1], None, AL.mult),
                ]), waits=[e_ex], event=E())
                e_dm = OP("tensor", lambda t, ti=ti, gt_i=gt_i: _last([
                    t.matmul(pb[3][0:8, 0:512], coeff_r[:, gt_i * 8:(gt_i + 1) * 8],
                             rv_tm[:, (ti % 2) * D:(ti % 2) * D + 512],
                             start=True, stop=True),
                    t.matmul(pb[4][0:8, 0:512], coeff_r[:, gt_i * 8:(gt_i + 1) * 8],
                             rv_tm[:, (ti % 2) * D + 512:(ti % 2 + 1) * D],
                             start=True, stop=True),
                    t.matmul(pb[3][32:48, 0:512], cmd_b[:, 0:16],
                             mv_tm[:, (ti % 2) * D:(ti % 2) * D + 512],
                             start=True, stop=True),
                    t.matmul(pb[4][32:48, 0:512], cmd_b[:, 0:16],
                             mv_tm[:, (ti % 2) * D + 512:(ti % 2 + 1) * D],
                             start=True, stop=True),
                ]), waits=[e_cf, e_gm, e_rvk, e_mvk] + st_upd[-1:], event=E())
                e_su = OP("vector", lambda v: _last([
                    v.tensor_tensor(st_r[0:8, 0:512], st_r[0:8, 0:512],
                                    pb[3][0:8, 0:512], AL.add),
                    v.tensor_tensor(st_r[0:8, 512:1024], st_r[0:8, 512:1024],
                                    pb[4][0:8, 0:512], AL.add),
                    v.scalar_tensor_tensor(st_m[0:16, 0:512], st_m[0:16, 0:512],
                                           DC, pb[3][32:48, 0:512], AL.mult, AL.add),
                    v.scalar_tensor_tensor(st_m[0:16, 512:1024], st_m[0:16, 512:1024],
                                           DC, pb[4][32:48, 0:512], AL.mult, AL.add),
                ]), waits=[e_dm], event=E())
                st_upd.append(e_su)
                pevs.append(e_su)
            proj_done[gg] = pevs

    e_rz = OP("vector", lambda v: _last([
        v.memset(st_r[:, :], 0.0),
        v.memset(st_m[:, :], 0.0),
    ]), waits=[st_upd[-1]], event=E("erz"))
    prevq = e_rz
    for qk in range(T // C):
        e_qlr = OP("sync", lambda s, qk=qk: s.dma_start(
            out=rv_tm[:, (qk % 2) * D:(qk % 2 + 1) * D],
            in_=dr["rv_s"][qk * C:(qk + 1) * C, :]),
            waits=[prevq], event=E(), dma=True, stream=f"erc{qk % 2}")
        e_qlm = OP("sync", lambda s, qk=qk: s.dma_start(
            out=mv_tm[:, (qk % 2) * D:(qk % 2 + 1) * D],
            in_=dr["mv_s"][qk * C:(qk + 1) * C, :]),
            waits=[e_qlr], event=E(), dma=True, stream=f"erc{qk % 2}")
        e_qc = OP("vector", lambda v, qk=qk: v.tensor_scalar(
            cmd_b[:, 0:16], coeff_m[:, qk * 16:(qk + 1) * 16],
            decC_t[:, 0:1], None, AL.mult), waits=[e_qlm, prevq], event=E())
        e_qm = OP("tensor", lambda t, qk=qk: _last([
            t.matmul(pb[0][0:8, 0:512], coeff_r[:, qk * 8:(qk + 1) * 8],
                     rv_tm[:, (qk % 2) * D:(qk % 2) * D + 512],
                     start=True, stop=True),
            t.matmul(pb[1][0:8, 0:512], coeff_r[:, qk * 8:(qk + 1) * 8],
                     rv_tm[:, (qk % 2) * D + 512:(qk % 2 + 1) * D],
                     start=True, stop=True),
            t.matmul(pb[0][32:48, 0:512], cmd_b[:, 0:16],
                     mv_tm[:, (qk % 2) * D:(qk % 2) * D + 512],
                     start=True, stop=True),
            t.matmul(pb[1][32:48, 0:512], cmd_b[:, 0:16],
                     mv_tm[:, (qk % 2) * D + 512:(qk % 2 + 1) * D],
                     start=True, stop=True),
        ]), waits=[e_qc, e_qlm, prevq], event=E())
        prevq = OP("vector", lambda v: _last([
            v.tensor_tensor(st_r[0:8, 0:512], st_r[0:8, 0:512],
                            pb[0][0:8, 0:512], AL.add),
            v.tensor_tensor(st_r[0:8, 512:1024], st_r[0:8, 512:1024],
                            pb[1][0:8, 0:512], AL.add),
            v.scalar_tensor_tensor(st_m[0:16, 0:512], st_m[0:16, 0:512],
                                   DC, pb[0][32:48, 0:512], AL.mult, AL.add),
            v.scalar_tensor_tensor(st_m[0:16, 512:1024], st_m[0:16, 512:1024],
                                   DC, pb[1][32:48, 0:512], AL.mult, AL.add),
        ]), waits=[e_qm], event=E())
    e_msk = OP("vector", lambda v: v.tensor_scalar(
        qy[0:8, 0:D], st_r[0:8, :], flags_t[0:8, 0:1], None, AL.mult),
        waits=[prevq], event=E("c1m"))
    e_ci0 = OP("gpsimd", lambda g: g.dma_start(out=dr["cc1_in"][0:8, :],
                                               in_=qy[0:8, 0:D]),
               waits=[e_msk], event=E("c1i0"), dma=True, stream="cc")
    e_msk2 = OP("vector", lambda v: v.tensor_scalar(
        qy[0:16, 0:D], st_m[0:16, :], flags_t[0:16, 0:1], None, AL.mult),
        waits=[e_ci0], event=E("c1m2"))
    e_ci = OP("gpsimd", lambda g: g.dma_start(out=dr["cc1_in"][8:24, :],
                                              in_=qy[0:16, 0:D]),
              waits=[e_msk2], event=E("c1i"), dma=True, stream="cc")
    e_cc = OP("gpsimd", lambda g: g.collective_compute(
        "AllReduce", AL.add, ins=[dr["cc1_in"][:, :]], outs=[dr["cc1_out"][:, :]],
        replica_groups=[[0, 1], [2, 3], [4, 5], [6, 7]]),
        waits=[e_ci], event=E("c1c"))
    e_co0 = OP("sync", lambda s: s.dma_start(out=qy[0:8, 0:D],
                                             in_=dr["cc1_out"][0:8, :]),
               waits=[e_cc], event=E("c1o0"), dma=True, stream="ccl")
    e_st00 = OP("vector", lambda v: _last([
        v.tensor_scalar(st_r[0:8, :], qy[0:8, 0:D], flags_t[0:8, 1:2], None, AL.mult),
        v.tensor_copy(st_rb[0:8, :], st_r[0:8, :]),
    ]), waits=[e_co0], event=E("c1s0"))
    e_co = OP("sync", lambda s: s.dma_start(out=qy[0:16, 0:D],
                                            in_=dr["cc1_out"][8:24, :]),
              waits=[e_st00], event=E("c1o"), dma=True, stream="ccl")
    e_st0 = OP("vector", lambda v: _last([
        v.tensor_scalar(st_m[0:16, :], qy[0:16, 0:D], flags_t[0:16, 1:2],
                        None, AL.mult),
        v.tensor_copy(st_mb[0:16, :], st_m[0:16, :]),
    ]), waits=[e_co], event=E("c1s"))
    e_stf = OP("tensor", lambda t: _last(
        [t.transpose(pb[3][:, k * 24:k * 24 + 8], st_r[0:8, k * 128:(k + 1) * 128],
                     ident[0:8, 0:8]) for k in range(KD)] +
        [t.transpose(pb[3][:, k * 24 + 8:k * 24 + 24],
                     st_m[0:16, k * 128:(k + 1) * 128],
                     ident[0:16, 0:16]) for k in range(KD)]),
        waits=[e_st0, e_st00], event=E("c1t"))
    e_stf2 = OP("vector", lambda v: _last([
        v.tensor_copy(st_fm[:, :], pb[3][:, 0:KD * 24]),
        v.tensor_copy(st_fmb[:, :], pb[3][:, 0:KD * 24]),
    ]), waits=[e_stf], event=E("c1e"))
    e_dbg3a = OP("gpsimd", lambda g: g.dma_start(out=dr["dbg3"][0:8, :],
                                                 in_=st_r[0:8, :]),
                 waits=[e_stf2], event=E(), dma=True, stream="dbg")
    e_dbg3b = OP("gpsimd", lambda g: g.dma_start(out=dr["dbg3"][8:24, :],
                                                 in_=st_m[0:16, :]),
                 waits=[e_stf2], event=E(), dma=True, stream="dbg")
    e_dbg3 = OP("vector", lambda v: v.engine_nop(), waits=[e_dbg3a, e_dbg3b],
                event=E())
    sweep_state = [e_dbg3]

    x2_done = {}
    q2t = {}
    xq2_grp_evs = {gq: [] for gq in range(NG)}
    ffn2_out = {}
    ffn2_tail = [prev_ffn_tail]
    ln_x_add = {}
    p2_loads = {}

    def p2_load(g):
        deps = [x2_done[g - 2][-1]] if g - 2 in x2_done else []
        if g - 2 in ffn2_out:
            deps = deps + [ffn2_out[g - 2][-1]]
        if g < 2:
            deps = deps + [proj_done[NG - 1][-1], st_upd[-1]]
        evs = {}
        evs["x1"] = OP("sync", lambda s, g=g: s.dma_start(
            out=xg[g % 2][:, :].rearrange("p (a b) -> p a b", a=4),
            in_=dr["x1_s"][g * G:(g + 1) * G, :].rearrange("(a p) b -> p a b", p=128)),
            waits=deps + [ffn1_done[g][-1]], event=E(f"p2x{g}"), dma=True, stream=f"px{g % 2}")
        evs["proj"] = proj_done[g][-1]
        for nm, bufm in [("rv_s", rv_fmt), ("mv_s", mv_fmt),
                         ("rq_s", rq_fmt), ("mq_s", mq_fmt)]:
            lastn = None
            for k in range(KD):
                lastn = OP("scalar", lambda s, nm=nm, bufm=bufm, k=k, g=g:
                           s.dma_start_transpose(
                    bufm[:, k * G:(k + 1) * G],
                    dr[nm][g * G:(g + 1) * G, k * 128:(k + 1) * 128]),
                    waits=[evs["x1"], evs["proj"]] +
                          ([sweep_done[g - 1][-1]] if g >= 1 else [])
                    if k == 0 and nm == "rv_s" else [lastn2],
                    event=E(), dma=True, stream="fmt")
                lastn2 = lastn
            evs[nm + "f"] = lastn
        evs["last"] = lastn
        return evs

    lastn2 = None
    sweep_done = {}
    out_stores = []

    def emit_ffn2(gg):
        buf = xg[gg % 2]

        def out_write(ti, pw, deps):
            if out_stores:
                deps = list(deps) + [out_stores[-1]]
            e = OP("vector", lambda v, ti=ti: _last([
                v.scalar_tensor_tensor(z1[:, 0:512], pw[:, 0:512],
                                       dq2sv[:, ti:ti + 1],
                                       buf[:, ti * D:ti * D + 512], AL.mult, AL.add),
                v.scalar_tensor_tensor(z1[:, 512:1024], pw[:, 512:1024],
                                       dq2sv[:, ti:ti + 1],
                                       buf[:, ti * D + 512:(ti + 1) * D],
                                       AL.mult, AL.add),
            ]), waits=deps, event=E(f"f2o{gg}{ti}"))
            e_s = OP("gpsimd", lambda gp, ti=ti, gg=gg: gp.dma_start(
                out=dr["out"][gg * G + ti * 128:gg * G + (ti + 1) * 128, :],
                in_=z1[:, 0:D]), waits=[e], event=E(f"os{gg}{ti}"), dma=True, stream="outs")
            out_stores.append(e_s)
            return e_s
        oevs = ffn_group(f"F2_{gg}", xq_fm[gg % 2], xq2_grp_evs[gg], out_write,
                         prev_class_evs=ffn2_tail[-1:])
        ffn2_tail.append(oevs[-1])
        return oevs

    for g in range(NG):
        ld = p2_load(g)
        p2_loads[g] = ld
        tile_out_evs = []
        for k in range(NCH):
            gt_i = g * 4 + k
            cs = k * 128
            dep_ld = [ld["last"], ld["x1"]]
            hw = (k % 2) * D
            e_lrv = OP("sync", lambda s, k=k, g=g: s.dma_start(
                out=rv_tm[:, (k % 2) * D:(k % 2 + 1) * D],
                in_=dr["rv_s"][g * G + k * 128:g * G + (k + 1) * 128, :]),
                waits=[ld["x1"], ld["proj"]] + sweep_state[-1:], event=E(), dma=True, stream="rvl")
            e_lmv = OP("sync", lambda s, k=k, g=g: s.dma_start(
                out=mv_tm[:, (k % 2) * D:(k % 2 + 1) * D],
                in_=dr["mv_s"][g * G + k * 128:g * G + (k + 1) * 128, :]),
                waits=[e_lrv], event=E(), dma=True, stream="mvl")
            e_lpre = OP("sync", lambda s, k=k, g=g: s.dma_start(
                out=pre_t[:, (k % 2) * D:(k % 2 + 1) * D],
                in_=dr["pre_s"][g * G + k * 128:g * G + (k + 1) * 128, :]),
                waits=[e_lmv], event=E(), dma=True, stream="prel")
            e_h = OP("vector", lambda v, k=k, gt_i=gt_i, g=g: v.tensor_scalar(
                h_tm[:, 0:D], xg[g % 2][:, k * D:(k + 1) * D],
                statsv[:, gt_i * 2:gt_i * 2 + 1],
                statsv[:, gt_i * 2 + 1:gt_i * 2 + 2], AL.mult, AL.add),
                waits=dep_ld + sweep_state[-1:], event=E())
            e_ct = OP("tensor", lambda t, gt_i=gt_i: _last([
                t.transpose(pbT[0:8, 512:640], coeff_r[:, gt_i * 8:(gt_i + 1) * 8],
                            identb[:, :]),
                t.transpose(pbT[0:16, 640:768], coeff_m[:, gt_i * 16:(gt_i + 1) * 16],
                            identb[:, :]),
            ]), waits=dep_ld + sweep_state[-1:] +
                ([last_pbT[0]] if last_pbT[0] else []), event=E())
            e_ce = OP("vector", lambda v: _last([
                v.tensor_copy(crf[0:8, :], pbT[0:8, 512:640]),
                v.tensor_copy(cmf[0:16, :], pbT[0:16, 640:768]),
            ]), waits=[e_ct], event=E())
            last_pbT[0] = e_ce
            e_am = OP("tensor", lambda t, cs=cs: _last(
                [t.matmul(pb[0][:, 0:128], rv_fmt[:, kd * G + cs:kd * G + cs + 128],
                          rq_fmt[:, kd * G + cs:kd * G + cs + 128],
                          start=(kd == 0), stop=(kd == KD - 1)) for kd in range(KD)] +
                [t.matmul(pb[0][:, 128:256], mv_fmt[:, kd * G + cs:kd * G + cs + 128],
                          mq_fmt[:, kd * G + cs:kd * G + cs + 128],
                          start=(kd == 0), stop=(kd == KD - 1)) for kd in range(KD)]),
                waits=[e_ce, ld["last"]], event=E())
            e_ae = OP("vector", lambda v: _last([
                v.tensor_tensor(pool2k[:, 512:640], pb[0][:, 0:128], minc_t[:, :], AL.mult),
                v.tensor_tensor(pool2k[:, 640:768], pb[0][:, 128:256], mdec_t[:, :],
                                AL.mult),
            ]), waits=[e_am], event=E())
            e_sm = OP("tensor", lambda t, cs=cs, gt_i=gt_i: _last(
                [t.matmul(pb[1][:, 0:8], rq_fmt[:, kd * G + cs:kd * G + cs + 128],
                          st_fmb[:, kd * 24:kd * 24 + 8],
                          start=(kd == 0), stop=False) for kd in range(KD)] +
                [t.matmul(pb[1][:, 8:24], mq_fmt[:, kd * G + cs:kd * G + cs + 128],
                          st_fmb[:, kd * 24 + 8:kd * 24 + 24],
                          start=False, stop=False) for kd in range(KD)] +
                [t.matmul(pb[1][:, 0:8], pool2k[:, 512:640],
                          coeff_r[:, gt_i * 8:(gt_i + 1) * 8], start=False, stop=False),
                 t.matmul(pb[1][:, 24:40], pool2k[:, 640:768],
                          coeff_m[:, gt_i * 16:(gt_i + 1) * 16], start=False, stop=True)]),
                waits=[e_ae] + sweep_state[-1:], event=E())
            e_scm = OP("vector", lambda v: _last([
                v.tensor_copy(sc_m[:, 0:16], pb[1][:, 8:24]),
                v.scalar_tensor_tensor(sc_m[:, 0:16], sc_m[:, 0:16], dec1_t[:, 0:1],
                                       pb[1][:, 24:40], AL.mult, AL.add),
            ]), waits=[e_sm], event=E())
            e_mx = OP("vector", lambda v: _last([
                v.tensor_reduce(stt[:, 24:25], pb[1][:, 0:8], AX, AL.max),
                v.tensor_scalar(stt[:, 24:25], stt[:, 24:25], -SCL, None, AL.mult),
                v.tensor_reduce(stt[:, 25:26], sc_m[:, 0:16], AX, AL.max),
                v.tensor_scalar(stt[:, 25:26], stt[:, 25:26], -SCL, None, AL.mult),
            ]), waits=[e_scm], event=E())
            e_ex = OP("scalar", lambda s: _last([
                s.activation(p_tm[:, 0:8], pb[1][:, 0:8], AF.Exp,
                             bias=stt[:, 24:25], scale=SCL, accum_out=stt[:, 26:27]),
                s.activation(p_tm[:, 8:24], sc_m[:, 0:16], AF.Exp,
                             bias=stt[:, 25:26], scale=SCL, accum_out=stt[:, 27:28]),
            ]), waits=[e_mx], event=E())
            e_pn = OP("vector", lambda v: _last([
                v.reciprocal(stt[:, 26:27], stt[:, 26:27]),
                v.reciprocal(stt[:, 27:28], stt[:, 27:28]),
                v.tensor_scalar(p_tm[:, 0:8], p_tm[:, 0:8], stt[:, 26:27], None, AL.mult),
                v.tensor_scalar(p_tm[:, 8:24], p_tm[:, 8:24], stt[:, 27:28],
                                None, AL.mult),
                v.tensor_scalar(pms[:, 0:16], p_tm[:, 8:24], dec1_t[:, 0:1],
                                None, AL.mult),
            ]), waits=[e_ex], event=E())
            e_pc = OP("vector", lambda v: _last([
                v.tensor_copy(pbf[:, 0:24], p_tm[:, 0:24]),
                v.tensor_copy(pmsbf[:, 0:16], pms[:, 0:16]),
            ]), waits=[e_pn], event=E())
            e_pt = OP("tensor", lambda t: _last([
                t.transpose(pbT[0:24, 512:640], pbf[:, 0:24], identb[:, :]),
                t.transpose(pbT[0:16, 640:768], pmsbf[:, 0:16], identb[:, :]),
                t.transpose(pbT[0:16, 768:896], pbf[:, 8:24], identb[:, :]),
            ]), waits=[e_pc] + ([last_pbT[0]] if last_pbT[0] else []), event=E())
            e_pe = OP("vector", lambda v: _last([
                v.tensor_copy(p_fm[0:24, :], pbT[0:24, 512:640]),
                v.tensor_copy(pms_fm[0:16, :], pbT[0:16, 640:768]),
                v.tensor_copy(pmr_fm[0:16, :], pbT[0:16, 768:896]),
            ]), waits=[e_pt], event=E())
            last_pbT[0] = e_pe
            e_wm = OP("tensor", lambda t, gt_i=gt_i: _last([
                t.matmul(pb[1][:, 128:256], crf[0:8, :], p_fm[0:8, :],
                         start=True, stop=True),
                t.matmul(pb[1][:, 256:384], cmf[0:16, :], pmr_fm[0:16, :],
                         start=True, stop=True),
            ]), waits=[e_pe, e_ce], event=E())
            e_we = OP("vector", lambda v: _last([
                v.tensor_tensor(pool2k[:, 768:896], pb[1][:, 128:256], minc_t[:, :], AL.mult),
                v.tensor_tensor(pool2k[:, 896:1024], pb[1][:, 256:384], mdec_t[:, :],
                                AL.mult),
            ]), waits=[e_wm], event=E())
            e_prev = [e_we]
            e_cmm = []
            for wv in range(4):
                jlist = list(range(wv * 4, wv * 4 + 4))
                e_rm = OP("tensor", lambda t, jlist=jlist, k=k, hw=hw: _last([
                    _last([
                        t.matmul(pb[2][:, (jc % 4) * 128:(jc % 4) * 128 + 128],
                                 st_rb[0:8, (jc % 8) * 128:(jc % 8) * 128 + 128]
                                 if jc < 8 else
                                 st_mb[0:16, (jc % 8) * 128:(jc % 8) * 128 + 128],
                                 p_fm[0:8, :] if jc < 8 else pms_fm[0:16, :],
                                 start=True, stop=False),
                        t.matmul(pb[2][:, (jc % 4) * 128:(jc % 4) * 128 + 128],
                                 rv_tm[:, hw + (jc % 8) * 128:
                                       hw + (jc % 8) * 128 + 128]
                                 if jc < 8 else
                                 mv_tm[:, hw + (jc % 8) * 128:
                                       hw + (jc % 8) * 128 + 128],
                                 pool2k[:, 768:896] if jc < 8 else pool2k[:, 896:1024],
                                 start=False, stop=True)])
                    for jc in jlist]), waits=e_prev + [e_lrv, e_lmv], event=E())
                e_re = OP("vector", lambda v: v.tensor_copy(
                    reads_sb[:, 0:512], pb[2][:, 0:512]), waits=[e_rm], event=E())
                e_prev = [e_re]
                e_lw = OP("sync", lambda s, wv=wv: s.dma_start(
                    out=w2str[:, (wv % 2) * 4 * D:(wv % 2) * 4 * D + 4 * D].rearrange(
                        "p (k n) -> p k n", n=D),
                    in_=dr["wrm_s"][wv * 512:(wv + 1) * 512, :].rearrange(
                        "(k p) n -> p k n", p=128)),
                    waits=[e_wprep, shared_w[0]] +
                          ([e_cmm[-2]] if len(e_cmm) >= 2 else []),
                    event=E(), dma=True, stream=f"wrm{wv % 2}")
                e_cm = OP("tensor", lambda t, wv=wv: _last([
                    _last([t.matmul(pb[0][:, 0:512],
                                    reads_sb[:, j * 128:(j + 1) * 128],
                                    w2str[:, (wv % 2) * 4 * D + j * D:
                                          (wv % 2) * 4 * D + j * D + 512],
                                    start=(wv == 0 and j == 0), stop=False),
                           t.matmul(pb[1][:, 0:512],
                                    reads_sb[:, j * 128:(j + 1) * 128],
                                    w2str[:, (wv % 2) * 4 * D + j * D + 512:
                                          (wv % 2) * 4 * D + (j + 1) * D],
                                    start=(wv == 0 and j == 0),
                                    stop=(wv == 3 and j == 3))])
                    for j in range(4)]), waits=[e_re, e_lw], event=E())
                e_cmm.append(e_cm)
                shared_w[0] = e_cm
            e_cph = OP("vector", lambda v, k=k, hw=hw: _last([
                v.tensor_tensor(h_tm[:, 0:D], h_tm[:, 0:D],
                                pre_t[:, hw:hw + D], AL.add),
                v.tensor_tensor(h_tm[:, 0:512], h_tm[:, 0:512],
                                pb[0][:, 0:512], AL.add),
                v.tensor_tensor(h_tm[:, 512:1024], h_tm[:, 512:1024],
                                pb[1][:, 0:512], AL.add),
            ]), waits=[e_cmm[-1], e_h, e_lpre], event=E())
            e_l1 = OP("scalar", lambda s: _last([
                s.activation(pb[2][:, 0:512], h_tm[:, 0:512], AF.Copy, bias=0.0,
                             scale=1.0 / D, accum_out=stt[:, 24:25]),
                s.activation(pb[2][:, 0:512], h_tm[:, 512:1024], AF.Copy, bias=0.0,
                             scale=1.0 / D, accum_out=stt[:, 31:32]),
            ]), waits=[e_cph], event=E())
            e_l2 = OP("scalar", lambda s: _last([
                s.activation(pb[2][:, 0:512], h_tm[:, 0:512], AF.Square, bias=0.0,
                             scale=1.0, accum_out=stt[:, 25:26]),
                s.activation(pb[2][:, 0:512], h_tm[:, 512:1024], AF.Square, bias=0.0,
                             scale=1.0, accum_out=stt[:, 30:31]),
            ]), waits=[e_cph, e_l1], event=E())
            e_l3 = OP("vector", lambda v: _last([
                v.tensor_add(stt[:, 24:25], stt[:, 24:25], stt[:, 31:32]),
                v.tensor_add(stt[:, 25:26], stt[:, 25:26], stt[:, 30:31]),
                v.tensor_tensor(stt[:, 26:27], stt[:, 24:25], stt[:, 24:25], AL.mult),
                v.scalar_tensor_tensor(stt[:, 27:28], stt[:, 25:26], 1.0 / D,
                                       stt[:, 26:27], AL.mult, AL.subtract),
            ]), waits=[e_l1, e_l2], event=E())
            e_l4 = OP("scalar", lambda s: s.activation(
                stt[:, 28:29], stt[:, 27:28], AF.Sqrt, bias=flags_t[:, 2:3], scale=1.0),
                waits=[e_l3], event=E())
            e_l5 = OP("vector", lambda v: _last([
                v.reciprocal(stt[:, 29:30], stt[:, 28:29]),
                v.scalar_tensor_tensor(stt[:, 30:31], stt[:, 24:25], -1.0,
                                       stt[:, 29:30], AL.mult, AL.mult),
            ]), waits=[e_l4], event=E())
            e_z1 = OP("vector", lambda v: v.tensor_scalar(
                z1[:, 0:D], h_tm[:, 0:D], stt[:, 29:30], stt[:, 30:31],
                AL.mult, AL.add), waits=[e_l5], event=E())
            e_z2 = OP("vector", lambda v: v.tensor_tensor(
                z1[:, 0:D], z1[:, 0:D], lng_b[:, 0:D], AL.mult),
                waits=[e_z1], event=E())
            e_z3 = OP("vector", lambda v: v.tensor_tensor(
                z1[:, 0:D], z1[:, 0:D], lnb_b[:, 0:D], AL.add),
                waits=[e_z2], event=E())
            e_x2 = OP("vector", lambda v, k=k, g=g: v.tensor_tensor(
                xg[g % 2][:, k * D:(k + 1) * D], xg[g % 2][:, k * D:(k + 1) * D],
                z1[:, 0:D], AL.add), waits=[e_z3, e_h], event=E())
            tile_out_evs.append(e_x2)
            e_cd = OP("vector", lambda v, gt_i=gt_i: _last([
                v.tensor_scalar(cmd_b[:, 0:16], coeff_m[:, gt_i * 16:(gt_i + 1) * 16],
                                decC_t[:, 0:1], None, AL.mult),
            ]), waits=[e_ce], event=E())
            e_dn = OP("tensor", lambda t, k=k, gt_i=gt_i, hw=hw: _last([
                t.matmul(pb[2][0:8, 0:512], coeff_r[:, gt_i * 8:(gt_i + 1) * 8],
                         rv_tm[:, hw:hw + 512], start=True, stop=True),
                t.matmul(pb[0][0:8, 0:512], coeff_r[:, gt_i * 8:(gt_i + 1) * 8],
                         rv_tm[:, hw + 512:hw + D], start=True, stop=True),
                t.matmul(pb[2][32:48, 0:512], cmd_b[:, 0:16],
                         mv_tm[:, hw:hw + 512], start=True, stop=True),
                t.matmul(pb[0][32:48, 0:512], cmd_b[:, 0:16],
                         mv_tm[:, hw + 512:hw + D], start=True, stop=True),
            ]), waits=[e_cd, e_cph, e_x2], event=E())
            e_df = OP("tensor", lambda t, k=k, hw=hw, g=g: _last([
                t.matmul(pb[1][:, 128 + kd * 24:128 + kd * 24 + 8],
                         rv_tm[:, hw + kd * 128:hw + (kd + 1) * 128],
                         coeff_r[:, (g * 4 + k) * 8:(g * 4 + k + 1) * 8],
                         start=True, stop=True) for kd in range(KD)] + [
                t.matmul(pb[1][:, 128 + kd * 24 + 8:128 + kd * 24 + 24],
                         mv_tm[:, hw + kd * 128:hw + (kd + 1) * 128],
                         cmd_b[:, 0:16], start=True, stop=True) for kd in range(KD)]),
                waits=[e_dn], event=E())
            e_sun = OP("vector", lambda v: _last([
                v.tensor_tensor(st_r[0:8, 0:512], st_r[0:8, 0:512],
                                pb[2][0:8, 0:512], AL.add),
                v.tensor_tensor(st_r[0:8, 512:1024], st_r[0:8, 512:1024],
                                pb[0][0:8, 0:512], AL.add),
                v.scalar_tensor_tensor(st_m[0:16, 0:512], st_m[0:16, 0:512], DC,
                                       pb[2][32:48, 0:512], AL.mult, AL.add),
                v.scalar_tensor_tensor(st_m[0:16, 512:1024], st_m[0:16, 512:1024],
                                       DC, pb[0][32:48, 0:512], AL.mult, AL.add),
                v.tensor_copy(st_rb[0:8, :], st_r[0:8, :]),
                v.tensor_copy(st_mb[0:16, :], st_m[0:16, :]),
            ]), waits=[e_df], event=E())
            e_suf = OP("vector", lambda v: _last([
                v.tensor_tensor(
                    st_fm[:, :].rearrange("p (kd n) -> p kd n", n=24)[:, :, 0:8],
                    st_fm[:, :].rearrange("p (kd n) -> p kd n", n=24)[:, :, 0:8],
                    pb[1][:, 128:320].rearrange("p (kd n) -> p kd n", n=24)[:, :, 0:8],
                    AL.add),
                v.scalar_tensor_tensor(
                    st_fm[:, :].rearrange("p (kd n) -> p kd n", n=24)[:, :, 8:24],
                    st_fm[:, :].rearrange("p (kd n) -> p kd n", n=24)[:, :, 8:24],
                    DC,
                    pb[1][:, 128:320].rearrange("p (kd n) -> p kd n", n=24)[:, :, 8:24],
                    AL.mult, AL.add),
                v.tensor_copy(st_fmb[:, :], st_fm[:, :]),
            ]), waits=[e_df, e_sun], event=E())
            sweep_state.append(e_suf)
        sweep_done[g] = tile_out_evs

        if g == 0:
            e_sva0 = OP("gpsimd", lambda gp: gp.dma_start(
                out=dr["cc2_out"][0:1, :], in_=xg[0][0:1, 0:D]),
                waits=[tile_out_evs[0]], event=E("sv0a"), dma=True, stream="cc")
            e_sva1 = OP("sync", lambda s: s.dma_start(
                out=rowsA[32:33, 0:D], in_=dr["cc2_out"][0:1, :]),
                waits=[e_sva0], event=E("sv0b"), dma=True, stream="ccl")
            e_sva2 = OP("vector", lambda v: v.tensor_scalar(
                rowsA[32:33, 0:D], rowsA[32:33, 0:D], flags_t[32:33, 1:2],
                None, AL.mult), waits=[e_sva1], event=E("sv0c"))
            e_sva = OP("gpsimd", lambda gp: gp.dma_start(
                out=dr["cc2_in"][1:2, :], in_=rowsA[32:33, 0:D]),
                waits=[e_sva2], event=E("sv0"), dma=True, stream="cc")
        if g == NG - 1:
            e_svb0 = OP("gpsimd", lambda gp: gp.dma_start(
                out=dr["cc2_out"][0:1, :], in_=xg[(NG - 1) % 2][127:128, 3 * D:4 * D]),
                waits=[tile_out_evs[3], e_sva], event=E("sv1a"), dma=True, stream="cc")
            e_svb1 = OP("sync", lambda s: s.dma_start(
                out=rowsA[32:33, 0:D], in_=dr["cc2_out"][0:1, :]),
                waits=[e_svb0, e_sva2], event=E("sv1b"), dma=True, stream="ccl")
            e_svb2 = OP("vector", lambda v: v.tensor_scalar(
                rowsA[32:33, 0:D], rowsA[32:33, 0:D], flags_t[32:33, 0:1],
                None, AL.mult), waits=[e_svb1], event=E("sv1c"))
            e_svb = OP("gpsimd", lambda gp: gp.dma_start(
                out=dr["cc2_in"][0:1, :], in_=rowsA[32:33, 0:D]),
                waits=[e_svb2], event=E("sv1"), dma=True, stream="cc")

        for ti in range(4):
            dep = [tile_out_evs[ti]] + ([q2t[(g, ti - 1)]] if ti > 0 else
                                        ([q2t[(g - 1, 3)]] if g > 0 else
                                         [xq_grp_evs[NG - 1][-1]]))
            if ti == 0 and (g - 2) in ffn2_out:
                dep = dep + [ffn2_out[g - 2][-1]]
            e_q = quant_tile(f"q2_{g}_{ti}", 128, xg[g % 2][:, ti * D:(ti + 1) * D],
                             xqt[:, 0:D], dep)
            e_te = transpose_to_fm(f"t2_{g}_{ti}", 128, xqt[:, 0:D],
                                   xq_fm[g % 2], 1 + ti * 128, G + 2, [e_q])
            q2t[(g, ti)] = e_te
            xq2_grp_evs[g].append(e_te)
        x2_done[g] = list(tile_out_evs)
        if g > 0:
            e_h0 = OP("vector", lambda v, g=g: v.tensor_copy(
                xq_fm[g % 2].rearrange("p (kd w) -> p kd w", w=G + 2)[:, :, 0:1],
                xq_fm[(g - 1) % 2].rearrange("p (kd w) -> p kd w", w=G + 2)[:, :, 512:513]),
                waits=[q2t[(g, 0)], q2t[(g - 1, 3)]], event=E(f"2h0{g}"))
            e_h1 = OP("vector", lambda v, g=g: v.tensor_copy(
                xq_fm[(g - 1) % 2].rearrange("p (kd w) -> p kd w", w=G + 2)[:, :, 513:514],
                xq_fm[g % 2].rearrange("p (kd w) -> p kd w", w=G + 2)[:, :, 1:2]),
                waits=[q2t[(g, 0)]], event=E(f"2h1{g}"))
            xq2_grp_evs[g].append(e_h0)
            xq2_grp_evs[g - 1].append(e_h1)
        if g == 0:
            pass

        if g >= 1:
            gg = g - 1
            ffn2_out[gg] = emit_ffn2(gg)

    e_cc2 = OP("gpsimd", lambda gp: gp.collective_compute(
        "AllReduce", AL.add, ins=[dr["cc2_in"][:, :]], outs=[dr["cc2_out"][:, :]],
        replica_groups=[[0, 1], [2, 3], [4, 5], [6, 7]]),
        waits=[e_sva, e_svb], event=E("c2c"))
    e_leq = quant_edge("q2eL", dr["cc2_out"][0:1, :], [e_cc2], mask_col=1)
    e_let = transpose_to_fm("t2eL", 1, ebf[0:1, 0:D], xq_fm[0], 0, G + 2, [e_leq],
                            pr_free=[q2t[(1, 0)]] if NG > 1 else [])
    xq2_grp_evs[0].append(e_let)
    e_req = quant_edge("q2eR", dr["cc2_out"][1:2, :], [e_cc2, e_let], mask_col=0)
    e_ret = transpose_to_fm("t2eR", 1, ebf[0:1, 0:D], xq_fm[(NG - 1) % 2], 513,
                            G + 2, [e_req])
    xq2_grp_evs[NG - 1].append(e_ret)

    ffn2_out[NG - 1] = emit_ffn2(NG - 1)

    OP("gpsimd", lambda gp: gp.nop(), waits=[out_stores[-1]], event=E("fin"))

    block = es.enter_context(nc.Block())
    sems = {e: es.enter_context(nc.semaphore(f"s_{e}")) for e in Prog.ENGINES}
    dsems = {s: es.enter_context(nc.semaphore(f"d_{s}")) for s in P.dma_counts}
    P.emit(block, sems, dsems)
    es.close()
    return nc


def kernel(**inputs):
    i = lambda n: np.asarray(inputs[n], dtype=np.float32)
    u = i("L0_attn_u")
    B, L, _ = u.shape
    out = np.empty((B, L, D), dtype=np.float32)

    idn = np.eye(128, dtype=np.float32)
    ii = np.arange(C)
    minc = (ii[:, None] <= ii[None, :]).astype(np.float32)
    mdec = minc * (DECAY ** np.maximum(ii[None, :] - ii[:, None], 0)).astype(np.float32)
    dec1 = (DECAY ** (ii + 1)).astype(np.float32).reshape(C, 1)
    decC = (DECAY ** (C - 1 - ii)).astype(np.float32).reshape(C, 1)
    onesr = np.ones((1, 128), dtype=np.float32)
    sel = np.zeros((8, 5 * 128), dtype=np.float32)
    for bi in range(5):
        sel[bi, bi * 128:(bi + 1) * 128] = 1.0

    cw = i("L0_ffn_conv_w").reshape(HID, 3)
    cw_cols = np.zeros((128, NHC * 3), dtype=np.float32)
    cb_cols = np.zeros((128, NHC), dtype=np.float32)
    cb = i("L0_ffn_conv_b")
    for c in range(NHC):
        cw_cols[:, 3 * c:3 * c + 3] = cw[c * 128:(c + 1) * 128, :]
        cb_cols[:, c] = cb[c * 128:(c + 1) * 128]

    gT = np.concatenate([i("L0_attn_reg_gate_w"), i("L0_attn_reg_addr_w"),
                         i("L0_attn_mem_gate_w"), i("L0_attn_mem_addr_w")],
                        axis=0).T
    gTp = np.ascontiguousarray(
        gT.reshape(KD, 128, 26).transpose(1, 0, 2).reshape(128, KD * 26))
    gateb = np.zeros((1, 32), dtype=np.float32)
    gateb[0, 0] = i("L0_attn_reg_gate_b")[0]
    gateb[0, 1:9] = i("L0_attn_reg_addr_b")
    gateb[0, 9] = i("L0_attn_mem_gate_b")[0]
    gateb[0, 10:26] = i("L0_attn_mem_addr_b")

    bias_rows = np.zeros((8, D), dtype=np.float32)
    bias_rows[0] = i("L0_attn_reg_val_b")
    bias_rows[1] = i("L0_attn_mem_val_b")
    bias_rows[2] = i("L0_attn_reg_q_b")
    bias_rows[3] = i("L0_attn_mem_q_b")
    bias_rows[4] = i("L0_attn_comb_b")
    bias_rows[5] = i("L0_attn_ln_g")
    bias_rows[6] = i("L0_attn_ln_b")

    comb = i("L0_attn_comb_w")
    whT = np.ascontiguousarray(comb[:, 0:D].T)
    wrmT = np.ascontiguousarray(
        np.concatenate([comb[:, D:2 * D].T, comb[:, 2 * D:3 * D].T], axis=0))
    w1T = np.ascontiguousarray(i("L0_ffn_w1").T)
    w2T = np.ascontiguousarray(i("L0_ffn_w2").T)
    wvT = np.ascontiguousarray(i("L0_attn_reg_val_w").T)
    wmvT = np.ascontiguousarray(i("L0_attn_mem_val_w").T)
    wqT = np.ascontiguousarray(i("L0_attn_reg_q_w").T)
    wmqT = np.ascontiguousarray(i("L0_attn_mem_q_w").T)

    nc = build_program()
    in_maps = []
    for core in range(8):
        b, hh = core // 2, core % 2
        s0 = hh * T
        u_pad = np.zeros((T + 2, D), dtype=np.float32)
        u_pad[1:T + 1] = u[b, s0:s0 + T]
        if s0 > 0:
            u_pad[0] = u[b, s0 - 1]
        if s0 + T < L:
            u_pad[T + 1] = u[b, s0 + T]
        flags = np.array([[1.0 - hh, float(hh), EPS, 0.0]], dtype=np.float32)
        in_maps.append({
            "u_pad": u_pad, "w1T": w1T, "w2T": w2T,
            "cw_cols": cw_cols, "cb_cols": cb_cols,
            "wvT": wvT, "wmvT": wmvT, "wqT": wqT, "wmqT": wmqT,
            "whT": whT, "wrmT": wrmT, "gTp": gTp,
            "bias_rows": bias_rows, "gateb_row": gateb,
            "ident": idn, "identb": idn.astype(ml_dtypes.bfloat16),
            "onesr": onesr, "selb": sel.astype(ml_dtypes.bfloat16),
            "minc": minc.astype(ml_dtypes.bfloat16), "mdec": mdec.astype(ml_dtypes.bfloat16),
            "dec1": dec1, "decC": decC, "flags": flags,
        })
    res = run_bass_kernel_spmd(nc, in_maps, core_ids=list(range(8)))
    for core in range(8):
        b, hh = core // 2, core % 2
        out[b, hh * T:(hh + 1) * T, :] = res.results[core]["out"]
    return out
